# version 17
# baseline (speedup 1.0000x reference)
import contextlib
import numpy as np

import concourse.bass as bass
import concourse.mybir as mybir
from concourse.bass_utils import run_bass_kernel_spmd

F32 = mybir.dt.float32
BF16 = mybir.dt.bfloat16
AF = mybir.ActivationFunctionType
ALU = mybir.AluOpType
AX = mybir.AxisListType
AP = bass.AP


class Cfg:
    def __init__(self, D=2048, TOK=1024, INTER=8192, WINP=16, stop=None):
        self.stop = stop
        self.D = D
        self.TOK = TOK
        self.INTER = INTER
        self.H = D // 128
        self.KT = D // 128
        self.NCH = TOK // 128
        self.NT = min(512, TOK)
        self.NN2 = TOK // self.NT
        self.PW = min(256, D)
        self.NPAN = D // self.PW
        self.OPW = 128
        self.NPAIR = INTER // self.OPW
        self.WINP = WINP
        self.NWIN = self.NPAIR // self.WINP
        self.WINK = (self.OPW * self.WINP) // 128
        self.OTN = self.OPW // 128
        self.ND = D // 512 if D >= 512 else 1
        self.EPS = 1e-5


def build_program(c: Cfg):
    nc = bass.Bass("TRN2", target_bir_lowering=False)
    D, TOK, INTER = c.D, c.TOK, c.INTER
    H, KT, NCH, NT, NN2 = c.H, c.KT, c.NCH, c.NT, c.NN2
    PW, NPAN, OPW, NPAIR = c.PW, c.NPAN, c.OPW, c.NPAIR
    WINP, NWIN, WINK, ND = c.WINP, c.NWIN, c.WINK, c.ND
    OTN = c.OTN
    HG = min(8, H)
    NHG = H // HG
    NGG = (OPW // 128) * NN2

    x_loc = nc.dram_tensor("x_loc", [TOK, D], F32, kind="ExternalInput").ap()
    x_car = nc.dram_tensor("x_car", [TOK, D], F32, kind="ExternalInput").ap()
    cflag = nc.dram_tensor("cflag", [1, 1], F32, kind="ExternalInput").ap()
    WSL0 = (D // 128) * c.PW
    wq = nc.dram_tensor("wq", [c.NPAN * 128, WSL0], F32, kind="ExternalInput").ap()
    wv = nc.dram_tensor("wv", [c.NPAN * 128, WSL0], F32, kind="ExternalInput").ap()
    wg = nc.dram_tensor("wg", [c.NPAN * 128, WSL0], F32, kind="ExternalInput").ap()
    wo = nc.dram_tensor("wo", [c.NPAN * 128, WSL0], F32, kind="ExternalInput").ap()
    GUSL0 = (D // 128) * c.OPW
    wgu_t = nc.dram_tensor("wgu_t", [2 * c.NPAIR * 128, GUSL0], F32,
                           kind="ExternalInput").ap()
    WDS0 = (c.WINK // 2) * 512
    wd_t = nc.dram_tensor("wd_t", [c.NWIN * c.ND * 2 * 128, WDS0], F32,
                          kind="ExternalInput").ap()
    beta16 = nc.dram_tensor("beta16", [1, H], F32, kind="ExternalInput").ap()
    lng_col = nc.dram_tensor("lng_col", [128, KT], F32, kind="ExternalInput").ap()
    lnb_col = nc.dram_tensor("lnb_col", [128, KT], F32, kind="ExternalInput").ap()
    out_loc = nc.dram_tensor("out_loc", [TOK, D], F32, kind="ExternalOutput").ap()

    es = contextlib.ExitStack()
    sbuf = lambda n, sh, dt: es.enter_context(nc.sbuf_tensor(n, sh, dt))
    psum = lambda n, sh, dt: es.enter_context(nc.psum_tensor(n, sh, dt))
    sem = lambda n: es.enter_context(nc.semaphore(n))

    tbuf = sbuf("tbuf", [128, KT * TOK], BF16)
    bbuf = sbuf("bbuf", [128, KT * TOK], BF16)
    qbuf = sbuf("qbuf", [128, KT * TOK], BF16)
    vbuf = sbuf("vbuf", [128, NCH * D], BF16)
    WSL = KT * PW
    GUSL = KT * OPW
    WDSL = (WINK // 2) * 512
    WD_OFF = 4 * GUSL
    WARENA = max(4 * WSL, WD_OFF + 2 * WDSL)
    warena = sbuf("warena", [128, WARENA], BF16)
    SFREE = max(8 * NT, 2 * D)
    sscr = sbuf("sscr", [128, SFREE], F32)
    decay = sbuf("decay", [128, H * 128], BF16)
    wpow = sbuf("wpow", [128, H * 128], BF16)
    wredT = sbuf("wredT", [128, NCH * H], BF16)
    ident = sbuf("ident", [128, 128], BF16)
    states = sbuf("states", [128, 2 * HG * 128], BF16)
    carry0 = sbuf("carry0", [128, H], F32)
    lngb = sbuf("lngb", [128, 2 * KT], F32)
    llrow = sbuf("llrow", [1, H + 1], F32)
    llbc = sbuf("llbc", [128, H + 1], F32)
    ones_r = sbuf("ones_r", [1, 128], F32)
    onesb_c = sbuf("onesb_c", [128, 1], BF16)
    zT = sbuf("zT", [128, KT * H], BF16)
    sqscr = sbuf("sqscr", [128, max(2 * TOK, D, 4 * NT)], BF16)
    mscr = sbuf("mscr", [128, 128], F32)
    epsc = sbuf("epsc", [128, 1], F32)
    tscr = sbuf("tscr", [128, 256], F32)

    zsb = sscr[0:H, 0:D]
    zbf = sscr[0:H, D:D + D // 2].bitcast(BF16)
    p = [psum(f"p{i}", [128, 512], F32) for i in range(7)]
    ptr = psum("ptr", [128, 1024], BF16)

    RS_OFF = 2 * NT
    NM_OFF = (2 + NN2) * NT
    B_OFF = 6 * NT
    SIL_OFF = 6 * NT

    sT, sV, sS, sP = sem("sT"), sem("sV"), sem("sS"), sem("sP")
    cnt = {"T": 0, "V": 0, "S": 0, "P": 0}
    semmap = {"T": sT, "V": sV, "S": sS, "P": sP}

    def tick(inst, e, k=1):
        inst.then_inc(semmap[e], k)
        cnt[e] += k
        return cnt[e]

    ds = {}
    for dn in ["xt2", "small", "x", "xh0", "xh1", "xld0", "xld1", "w0", "w1", "w2", "w3",
               "gu0", "gu1", "gu2", "gu3", "wd0", "wd1", "out0", "out1"]:
        ds[dn] = [sem("d_" + dn), 0]

    def dtick(inst, name):
        ds[name][1] += 16
        inst.then_inc(ds[name][0], 16)
        return (name, ds[name][1])

    def dwait(eng_obj, key):
        name, v = key
        eng_obj.wait_ge(ds[name][0], v)

    bank_w = {}
    bank_r = {}

    def wtb(eng_obj, idx):
        if idx in bank_r:
            e, n = bank_r[idx]
            eng_obj.wait_ge(semmap[e], n)

    def wrb(eng_obj, idx):
        e, n = bank_w[idx]
        eng_obj.wait_ge(semmap[e], n)

    blk = es.enter_context(nc.Block())
    mil = {}

    def dbg_finish():
        @blk.gpsimd
        def _(g):
            tick(g.memset(sscr[:, 0:D], 0.0), "P")
        @blk.sync
        def _(sy):
            sy.wait_ge(sP, cnt["P"])
            for ci2 in range(NCH):
                dtick(sy.dma_start(
                    AP(out_loc.tensor, ci2 * 128 * D, [[D, 128], [1, D]]),
                    AP(sscr, 0, [[SFREE, 128], [1, D]])), "out0")
            sy.wait_ge(ds["out0"][0], ds["out0"][1])
        es.close()
        return nc

    @blk.sync
    def _(sy):
        dtick(sy.dma_start(llrow[:1, :H], beta16), "small")
        dtick(sy.dma_start(llrow[:1, H:H + 1], cflag), "small")
        dtick(sy.dma_start(lngb[:, 0:KT], lng_col), "small")
        mil["small_in"] = dtick(sy.dma_start(lngb[:, KT:2 * KT], lnb_col),
                                "small")

    @blk.gpsimd
    def _(g):
        h2 = NCH // 2
        mil["x_bf0"] = dtick(g.dma_start(
            AP(bbuf, 0, [[KT * TOK, 128], [D, h2], [1, D]]),
            AP(x_loc.tensor, 0, [[D, 128], [128 * D, h2], [1, D]])), "xh0")
        g.iota(tscr[:, 0:128], [[1, 128]], base=0, channel_multiplier=-1,
               allow_small_or_imprecise_dtypes=True)
        g.iota(tscr[:, 128:256], [[1, 128]], base=1, channel_multiplier=0,
               allow_small_or_imprecise_dtypes=True)
        g.memset(ident[:, :], 1.0)
        g.memset(epsc[:, :], 1e-5)
        g.memset(ones_r[:1, :], 1.0)
        mil["iotas"] = tick(g.memset(onesb_c[:, :], 1.0), "P")
        g.wait_ge(sP, mil["iotas"])
        mil["ident"] = tick(g.affine_select(
            ident[:, :], ident[:, :], [[1, 128]], ALU.is_equal, 0.0,
            base=0, channel_multiplier=-1), "P")
        dwait(g, mil["x_bf0"])
        mil["x_bf"] = dtick(g.dma_start(
            AP(bbuf, h2 * D, [[KT * TOK, 128], [D, NCH - h2], [1, D]]),
            AP(x_loc.tensor, h2 * 128 * D,
               [[D, 128], [128 * D, NCH - h2], [1, D]])), "xh1")


    @blk.scalar
    def _(sc):
        dwait(sc, mil["small_in"])
        tick(sc.activation(llrow[:1, 0:H], llrow[:1, 0:H], AF.Sigmoid), "S")
        sc.wait_ge(sS, cnt["S"])
        mil["loglam"] = tick(sc.activation(llrow[:1, 0:H], llrow[:1, 0:H],
                                           AF.Ln), "S")

    @blk.tensor
    def _(t):
        t.wait_ge(sS, mil["loglam"])
        t.wait_ge(sP, mil["iotas"])
        mil["llmm"] = tick(t.matmul(p[6][:, 0:H + 1], ones_r[:1, :],
                                    llrow[:1, :], start=True, stop=True), "T")
    bank_w[6] = ("T", cnt["T"])

    @blk.vector
    def _(v):
        v.wait_ge(sT, mil["llmm"])
        mil["llbc"] = tick(v.tensor_copy(llbc[:, :], p[6][:, 0:H + 1]), "V")
    bank_r[6] = ("V", cnt["V"])


    tgroups = [(ci, dt) for ci in range(NCH) for dt in range(KT)]
    ngr = (len(tgroups) + 7) // 8

    def tp_bank(par):
        return ptr[:, :] if par == 0 else p[6][:, :].bitcast(BF16)

    for gi in range(ngr):
        tiles = tgroups[gi * 8:(gi + 1) * 8]

        @blk.tensor
        def _(t, gi=gi, tiles=tiles):
            if gi == 0:
                dwait(t, mil["x_bf0"])
                t.wait_ge(sP, mil["ident"])
            if tiles[0][0] >= NCH // 2 and (gi == 0 or tgroups[gi * 8 - 1][0] < NCH // 2):
                dwait(t, mil["x_bf"])
            if gi == 1:
                wtb(t, 6)
            if gi >= 2:
                t.wait_ge(sV, mil[("xev", gi - 2)])
            bank = tp_bank(gi % 2)
            for j, (ci, dt) in enumerate(tiles):
                inst = t.transpose(bank[:, j * 128:(j + 1) * 128],
                                   bbuf[:, ci * D + dt * 128:
                                        ci * D + (dt + 1) * 128],
                                   ident[:, :])
            mil[("xtr", gi)] = tick(inst, "T")

        @blk.vector
        def _(v, gi=gi, tiles=tiles):
            v.wait_ge(sT, mil[("xtr", gi)])
            bank = tp_bank(gi % 2)
            for j, (ci, dt) in enumerate(tiles):
                inst = v.tensor_copy(
                    tbuf[:, dt * TOK + ci * 128: dt * TOK + (ci + 1) * 128],
                    bank[:, j * 128:(j + 1) * 128])
            mil[("xev", gi)] = tick(inst, "V")
    mil["xT_done"] = cnt["V"]
    bank_r[6] = ("V", mil["xT_done"])
    if c.stop == "X":
        return dbg_finish()

    def panel_dma(g, which, pi, slot):
        src = {"q": wq, "v": wv, "g": wg, "o": wo}[which]
        return dtick(g.dma_start(
            AP(warena, slot * WSL, [[WARENA, 128], [1, WSL]]),
            AP(src.tensor, pi * 128 * WSL, [[WSL, 128], [1, WSL]])),
            "w%d" % slot)

    @blk.gpsimd
    def _(g):
        dwait(g, mil["x_bf"])
        for pi in range(min(2, NPAN)):
            mil[("wqp", pi)] = panel_dma(g, "q", pi, pi % 2)
        dwait(g, mil[("wqp", 0)])
        mil["x_car"] = dtick(g.dma_start(
            AP(vbuf, 0, [[NCH * D, 128], [D, NCH], [1, D]]),
            AP(x_car.tensor, 0, [[D, 128], [128 * D, NCH], [1, D]])), "x")

    @blk.vector
    def _(v):
        v.wait_ge(sV, mil["llbc"])
        v.wait_ge(sP, mil["ident"])
        for h in range(H):
            v.tensor_scalar_mul(sscr[:, h * 128:(h + 1) * 128],
                                tscr[:, 0:128], llbc[:, h:h + 1])
            inst = v.tensor_scalar_mul(sscr[:, SFREE // 2 + h * 128:SFREE // 2 + (h + 1) * 128],
                                       tscr[:, 128:256], llbc[:, h:h + 1])
        mil["dmul_all"] = tick(inst, "V")

    @blk.gpsimd
    def _(g):
        g.wait_ge(sV, mil["dmul_all"])
        for h in range(H):
            inst = g.affine_select(sscr[:, h * 128:(h + 1) * 128],
                                   sscr[:, h * 128:(h + 1) * 128], [[1, 128]],
                                   ALU.is_ge, -1e30, base=0,
                                   channel_multiplier=-1)
        mil["dmask_all"] = tick(inst, "P")

    @blk.scalar
    def _(sc):
        sc.wait_ge(sP, mil["dmask_all"])
        for h in range(H):
            sc.activation(decay[:, h * 128:(h + 1) * 128],
                          sscr[:, h * 128:(h + 1) * 128], AF.Exp)
            inst = sc.activation(wpow[:, h * 128:(h + 1) * 128],
                                 sscr[:, SFREE // 2 + h * 128:SFREE // 2 + (h + 1) * 128],
                                 AF.Exp)
        mil["decay_done"] = tick(inst, "S")

    @blk.gpsimd
    def _(g):
        g.wait_ge(sS, mil["decay_done"])
        for st in range(NCH):
            inst = g.iota(tscr[:, st:st + 1], [[1, 1]],
                          base=(TOK - 1) - st * 128, channel_multiplier=-1,
                          allow_small_or_imprecise_dtypes=True)
        mil["rcol_all"] = tick(inst, "P")

    @blk.vector
    def _(v):
        v.wait_ge(sP, mil["rcol_all"])
        for st in range(NCH):
            inst = v.tensor_scalar_mul(sscr[:, st * H:(st + 1) * H],
                                       llbc[:, 0:H], tscr[:, st:st + 1])
        mil["rmul_all"] = tick(inst, "V")

    @blk.scalar
    def _(sc):
        sc.wait_ge(sV, mil["rmul_all"])
        for st in range(NCH):
            inst = sc.activation(sscr[:, SFREE // 2 + st * H:SFREE // 2 + (st + 1) * H],
                                 sscr[:, st * H:(st + 1) * H], AF.Exp)
        mil["rexp_all"] = tick(inst, "S")

    @blk.vector
    def _(v):
        v.wait_ge(sS, mil["rexp_all"])
        for st in range(NCH):
            inst = v.tensor_scalar_mul(wredT[:, st * H:(st + 1) * H],
                                       sscr[:, SFREE // 2 + st * H:SFREE // 2 + (st + 1) * H],
                                       llbc[:, H:H + 1])
        mil["wred_done"] = tick(inst, "V")

    qgrp = 0
    NDT = PW // 128
    for pi in range(NPAN):
        slot = pi % 2
        if pi >= 2:
            @blk.gpsimd
            def _(g, pi=pi, slot=slot):
                g.wait_ge(sT, mil[("qpdone", pi - 2)])
                mil[("wqp", pi)] = panel_dma(g, "q", pi, slot)
        for dtl in range(NDT):
            dt = pi * NDT + dtl
            for n2 in range(NN2):
                bk = qgrp % 4

                @blk.tensor
                def _(t, pi=pi, slot=slot, dt=dt, dtl=dtl, n2=n2, bk=bk,
                      qgrp=qgrp):
                    if dtl == 0 and n2 == 0:
                        dwait(t, mil[("wqp", pi)])
                        if pi == 0:
                            t.wait_ge(sV, mil["xT_done"])
                    wtb(t, bk)
                    for k in range(KT):
                        inst = t.matmul(
                            p[bk][:, 0:NT],
                            warena[:, slot * WSL + k * PW + dtl * 128:
                                   slot * WSL + k * PW + (dtl + 1) * 128],
                            tbuf[:, k * TOK + n2 * NT: k * TOK + (n2 + 1) * NT],
                            start=(k == 0), stop=(k == KT - 1))
                    mil[("qmm", qgrp)] = tick(inst, "T")
                    bank_w[bk] = ("T", cnt["T"])
                    if dtl == NDT - 1 and n2 == NN2 - 1:
                        mil[("qpdone", pi)] = cnt["T"]

                @blk.vector
                def _(v, dt=dt, n2=n2, bk=bk, qgrp=qgrp):
                    v.wait_ge(sT, mil[("qmm", qgrp)])
                    tick(v.tensor_copy(
                        qbuf[:, dt * TOK + n2 * NT: dt * TOK + (n2 + 1) * NT],
                        p[bk][:, 0:NT]), "V")
                    mil[("qev", qgrp)] = cnt["V"]
                    bank_r[bk] = ("V", cnt["V"])
                qgrp += 1
    mil["q_done"] = cnt["V"]
    if c.stop == "Q":
        return dbg_finish()

    for n in range(D // PW):
        @blk.tensor
        def _(t, n=n):
            if n == 0:
                dwait(t, mil["x_car"])
                t.wait_ge(sV, mil["wred_done"])
            wtb(t, 5)
            for st in range(NCH):
                inst = t.matmul(p[5][0:H, 0:PW],
                                wredT[:, st * H:(st + 1) * H],
                                vbuf[:, st * D + n * PW: st * D + (n + 1) * PW],
                                start=(st == 0), stop=(st == NCH - 1))
            mil[("zmm", n)] = tick(inst, "T")
        bank_w[5] = ("T", cnt["T"])

        @blk.vector
        def _(v, n=n):
            v.wait_ge(sT, mil[("zmm", n)])
            tick(v.tensor_copy(zsb[:, n * PW:(n + 1) * PW],
                               p[5][0:H, 0:PW]), "V")
            mil[("zev", n)] = cnt["V"]
        bank_r[5] = ("V", cnt["V"])

    @blk.vector
    def _(v):
        v.wait_ge(sV, cnt["V"])
        tick(v.tensor_copy(zbf[:, :], zsb[:, :]), "V")
        mil["zbf"] = cnt["V"]

    nzg = (KT + 7) // 8
    for zg in range(nzg):
        dts = range(zg * 8, min(zg * 8 + 8, KT))

        @blk.tensor
        def _(t, zg=zg, dts=dts):
            if zg == 0:
                t.wait_ge(sV, mil["zbf"])
            wtb(t, 7 if zg % 2 == 0 else 6)
            if zg >= 2:
                t.wait_ge(sV, mil[("zTev", zg - 2)])
            bank = tp_bank(zg % 2)
            for dt in dts:
                inst = t.transpose(bank[:, (dt % 8) * 128:(dt % 8) * 128 + H],
                                   zbf[:, dt * 128:(dt + 1) * 128],
                                   ident[0:H, 0:H])
            mil[("ztr", zg)] = tick(inst, "T")

        @blk.vector
        def _(v, zg=zg, dts=dts):
            v.wait_ge(sT, mil[("ztr", zg)])
            bank = tp_bank(zg % 2)
            for dt in dts:
                inst = v.tensor_copy(zT[:, dt * H:(dt + 1) * H],
                                     bank[:, (dt % 8) * 128:(dt % 8) * 128 + H])
            mil[("zTev", zg)] = tick(inst, "V")
    mil["zT_done"] = cnt["V"]
    bank_r[7] = ("V", mil["zT_done"])
    bank_r[6] = ("V", mil["zT_done"])

    vgrp = 0
    for pi in range(NPAN):
        slot = 2 + pi % 2

        @blk.gpsimd
        def _(g, pi=pi, slot=slot):
            if pi >= 2:
                g.wait_ge(sT, mil[("vpdone", pi - 2)])
            mil[("wvp", pi)] = panel_dma(g, "v", pi, slot)

        if c.stop != "V0":
            @blk.tensor
            def _(t, pi=pi, slot=slot):
                dwait(t, mil[("wvp", pi)])
                t.wait_ge(sV, mil["zT_done"])
                wtb(t, 5)
                for k in range(KT):
                    inst = t.matmul(p[5][0:H, 0:PW], zT[:, k * H:(k + 1) * H],
                                    warena[:, slot * WSL + k * PW:
                                           slot * WSL + (k + 1) * PW],
                                    start=(k == 0), stop=(k == KT - 1))
                mil[("cmm", pi)] = tick(inst, "T")
                bank_w[5] = ("T", cnt["T"])

            @blk.vector
            def _(v, pi=pi):
                v.wait_ge(sT, mil[("cmm", pi)])
                tick(v.tensor_copy(zsb[:, pi * PW:(pi + 1) * PW],
                                   p[5][0:H, 0:PW]), "V")
                mil[("cev", pi)] = cnt["V"]
                bank_r[5] = ("V", cnt["V"])

        for ci in range(NCH):
            bk = vgrp % 4

            @blk.tensor
            def _(t, pi=pi, slot=slot, ci=ci, bk=bk, vgrp=vgrp):
                wtb(t, bk)
                for k in range(KT):
                    inst = t.matmul(
                        p[bk][:, 0:PW],
                        tbuf[:, k * TOK + ci * 128: k * TOK + (ci + 1) * 128],
                        warena[:, slot * WSL + k * PW: slot * WSL + (k + 1) * PW],
                        start=(k == 0), stop=(k == KT - 1))
                mil[("vmm", vgrp)] = tick(inst, "T")
                bank_w[bk] = ("T", cnt["T"])
                if ci == NCH - 1:
                    mil[("vpdone", pi)] = cnt["T"]

            @blk.vector
            def _(v, pi=pi, ci=ci, bk=bk, vgrp=vgrp):
                v.wait_ge(sT, mil[("vmm", vgrp)])
                tick(v.tensor_copy(
                    vbuf[:, ci * D + pi * PW: ci * D + (pi + 1) * PW],
                    p[bk][:, 0:PW]), "V")
                mil[("vev", vgrp)] = cnt["V"]
                bank_r[bk] = ("V", cnt["V"])
            vgrp += 1
    mil["v_done"] = cnt["V"]
    if c.stop in ("V0", "V1"):
        return dbg_finish()

    @blk.vector
    def _(v):
        v.wait_ge(sV, mil[("cev", NPAN - 1)])
        tick(v.tensor_copy(zbf[:, :], zsb[:, :]), "V")
        mil["carbf"] = cnt["V"]

    ncg = (H + 7) // 8
    for cg in range(ncg):
        hs = range(cg * 8, min(cg * 8 + 8, H))

        @blk.tensor
        def _(t, cg=cg, hs=hs):
            if cg == 0:
                t.wait_ge(sV, mil["carbf"])
            wtb(t, 7 if cg % 2 == 0 else 6)
            if cg >= 2:
                t.wait_ge(sV, mil[("c0ev", cg - 2)])
            bank = tp_bank(cg % 2)
            for h in hs:
                inst = t.transpose(bank[:, (h % 8) * 128:(h % 8) * 128 + H],
                                   zbf[:, h * 128:(h + 1) * 128],
                                   ident[0:H, 0:H])
            mil[("ctr", cg)] = tick(inst, "T")

        @blk.vector
        def _(v, cg=cg, hs=hs):
            v.wait_ge(sT, mil[("ctr", cg)])
            bank = tp_bank(cg % 2)
            for h in hs:
                inst = v.tensor_copy(
                    carry0[:, h:h + 1],
                    bank[:, (h % 8) * 128 + h:(h % 8) * 128 + h + 1])
            mil[("c0ev", cg)] = tick(inst, "V")
    mil["carry0_done"] = cnt["V"]
    bank_r[7] = ("V", mil["carry0_done"])
    bank_r[6] = ("V", mil["carry0_done"])
    if c.stop == "V":
        return dbg_finish()

    Ggroups = [(pi, dtl, n2) for pi in range(NPAN) for dtl in range(NDT)
               for n2 in range(NN2)]
    mil_pool_scan = {}
    gbanks = [0, 1, 6]
    ggrp_idx = [0]

    @blk.gpsimd
    def _(g):
        g.wait_ge(sT, mil[("qpdone", NPAN - 1)])
        mil[("wgp", 0)] = panel_dma(g, "g", 0, 0)
        if NPAN > 1:
            mil[("wgp", 1)] = panel_dma(g, "g", 1, 1)

    def emit_g_group():
        ggrp = ggrp_idx[0]
        if ggrp >= len(Ggroups):
            return
        ggrp_idx[0] += 1
        pi, dtl, n2 = Ggroups[ggrp]
        dt = pi * NDT + dtl
        slot = pi % 2
        bk = gbanks[ggrp % 3]
        first_of_panel = (dtl == 0 and n2 == 0)
        if first_of_panel and pi >= 2:
            @blk.gpsimd
            def _(g, pi=pi, slot=slot):
                g.wait_ge(sT, mil[("gpdone", pi - 2)])
                mil[("wgp", pi)] = panel_dma(g, "g", pi, slot)

        @blk.tensor
        def _(t, pi=pi, slot=slot, dtl=dtl, n2=n2, bk=bk, ggrp=ggrp,
              first_of_panel=first_of_panel):
            if first_of_panel:
                dwait(t, mil[("wgp", pi)])
            wtb(t, bk)
            for k in range(KT):
                inst = t.matmul(
                    p[bk][:, 0:NT],
                    warena[:, slot * WSL + k * PW + dtl * 128:
                           slot * WSL + k * PW + (dtl + 1) * 128],
                    tbuf[:, k * TOK + n2 * NT: k * TOK + (n2 + 1) * NT],
                    start=(k == 0), stop=(k == KT - 1))
            mil[("gmm", ggrp)] = tick(inst, "T")
            bank_w[bk] = ("T", cnt["T"])
            if dtl == NDT - 1 and n2 == NN2 - 1:
                mil[("gpdone", pi)] = cnt["T"]

        @blk.scalar
        def _(sc, dt=dt, n2=n2, bk=bk, ggrp=ggrp):
            sc.wait_ge(sT, mil[("gmm", ggrp)])
            if ggrp == 0:
                sc.wait_ge(sT, mil[("xtr", ngr - 1)])
            inst = sc.activation(
                bbuf[:, dt * TOK + n2 * NT: dt * TOK + (n2 + 1) * NT],
                p[bk][:, 0:NT], AF.Silu)
            mil[("gsil", ggrp)] = tick(inst, "S")
        bank_r[bk] = ("S", cnt["S"])

    for hg in range(NHG):
        for ci in range(NCH):
            par = ci % 2
            bks = (2, 3) if par == 0 else (4, 5)

            @blk.tensor
            def _(t, hg=hg, ci=ci, bks=bks):
                if hg == 0 and ci == 0:
                    t.wait_ge(sV, mil["v_done"])
                    t.wait_ge(sV, mil["carry0_done"])
                if hg == 0 and ci < 2:
                    for b in bks:
                        wtb(t, b)
                if ci >= 2:
                    t.wait_ge(sV, mil[("sdve", hg, ci - 2)])
                elif hg > 0:
                    t.wait_ge(sV, mil[("sdve", hg - 1, NCH - 2 + ci)])
                for hl in range(HG):
                    h = hg * HG + hl
                    bk = bks[hl // 4]
                    inst = t.matmul(
                        p[bk][:, (hl % 4) * 128:(hl % 4 + 1) * 128],
                        vbuf[:, ci * D + h * 128: ci * D + (h + 1) * 128],
                        decay[:, h * 128:(h + 1) * 128],
                        start=True, stop=True)
                mil[("smm", hg, ci)] = tick(inst, "T")
                for b in bks:
                    bank_w[b] = ("T", cnt["T"])

            @blk.vector
            def _(v, hg=hg, ci=ci, par=par, bks=bks):
                v.wait_ge(sT, mil[("smm", hg, ci)])
                if ci > 0:
                    v.wait_ge(sV, mil[("sdve", hg, ci - 1)])
                elif hg > 0:
                    v.wait_ge(sV, mil[("sdve", hg - 1, NCH - 1)])
                if (hg, ci - 2) in mil_pool_scan:
                    v.wait_ge(sV, mil_pool_scan[(hg, ci - 2)])
                elif hg > 0 and ci < 2:
                    v.wait_ge(sV, mil_pool_scan[(hg - 1, NCH - 2 + ci)])
                for hl in range(HG):
                    h = hg * HG + hl
                    bk = bks[hl // 4]
                    st_sl = states[:, (par * HG + hl) * 128:
                                   (par * HG + hl + 1) * 128]
                    if ci == 0:
                        car = carry0[:, h:h + 1]
                    else:
                        car = states[:, ((1 - par) * HG + hl) * 128 + 127:
                                     ((1 - par) * HG + hl) * 128 + 128]
                    inst = v.scalar_tensor_tensor(
                        st_sl, wpow[:, h * 128:(h + 1) * 128], car,
                        p[bk][:, (hl % 4) * 128:(hl % 4 + 1) * 128],
                        ALU.mult, ALU.add)
                tick(inst, "V")
                mil[("sdve", hg, ci)] = cnt["V"]
                for b in bks:
                    bank_r[b] = ("V", cnt["V"])
                v.wait_ge(sV, cnt["V"])
                if hg == 0 and ci == 0:
                    v.wait_ge(sV, mil["q_done"])
                for hl in range(HG):
                    h = hg * HG + hl
                    st_sl = states[:, (par * HG + hl) * 128:
                                   (par * HG + hl + 1) * 128]
                    ysl = qbuf[:, h * TOK + ci * 128: h * TOK + (ci + 1) * 128]
                    inst = v.tensor_mul(ysl, ysl, st_sl)
                mil_pool_scan[(hg, ci)] = tick(inst, "V")

            emit_g_group()
            emit_g_group()
    while ggrp_idx[0] < len(Ggroups):
        emit_g_group()
    mil["y_done"] = mil_pool_scan[(NHG - 1, NCH - 1)]
    mil["gsil_done"] = cnt["S"]
    if c.stop == "SCAN":
        return dbg_finish()

    for h in range(H):
        @blk.vector
        def _(v, h=h):
            if h == 0:
                v.wait_ge(sV, mil["y_done"])
            if h >= 2:
                v.wait_ge(sT, mil[("l1mm", h - 2)])
            ysl = qbuf[:, h * TOK:(h + 1) * TOK]
            inst = v.scalar_tensor_tensor(
                sqscr[:, (h % 2) * TOK:(h % 2) * TOK + TOK],
                ysl, 1.0, ysl, ALU.mult, ALU.mult)
            mil[("l1sq", h)] = tick(inst, "V")

        @blk.tensor
        def _(t, h=h):
            t.wait_ge(sV, mil[("l1sq", h)])
            if h == 0:
                for b in range(4):
                    wtb(t, b)
            for n2 in range(NN2):
                t.matmul(p[n2][0:1, 0:NT], onesb_c[:, 0:1],
                         qbuf[:, h * TOK + n2 * NT: h * TOK + (n2 + 1) * NT],
                         start=(h == 0), stop=(h == H - 1),
                         skip_group_check=True)
                inst = t.matmul(
                    p[2 + n2][0:1, 0:NT], onesb_c[:, 0:1],
                    sqscr[:, (h % 2) * TOK + n2 * NT:
                          (h % 2) * TOK + (n2 + 1) * NT],
                    start=(h == 0), stop=(h == H - 1), skip_group_check=True)
            mil[("l1mm", h)] = tick(inst, "T")
    mil["l1mm_done"] = cnt["T"]
    for b in range(4):
        bank_w[b] = ("T", mil["l1mm_done"])

    for n2 in range(NN2):
        mu = sscr[0:1, n2 * NT:(n2 + 1) * NT]
        var = sscr[0:1, B_OFF + n2 * NT: B_OFF + (n2 + 1) * NT]
        rs = sscr[0:1, RS_OFF + n2 * NT: RS_OFF + (n2 + 1) * NT]
        nm = sscr[0:1, NM_OFF + n2 * NT: NM_OFF + (n2 + 1) * NT]

        @blk.vector
        def _(v, n2=n2, mu=mu, var=var, rs=rs):
            v.wait_ge(sT, mil["l1mm_done"])
            v.wait_ge(sV, cnt["V"])
            tick(v.tensor_scalar_mul(mu, p[n2][0:1, 0:NT], 1.0 / D), "V")
            v.wait_ge(sV, cnt["V"])
            tick(v.tensor_scalar_mul(var, p[2 + n2][0:1, 0:NT], 1.0 / D), "V")
            v.wait_ge(sV, cnt["V"])
            tick(v.tensor_mul(rs, mu, mu), "V")
            v.wait_ge(sV, cnt["V"])
            mil[("l1var", n2)] = tick(v.tensor_sub(var, var, rs), "V")
        bank_r[n2] = ("V", cnt["V"])
        bank_r[2 + n2] = ("V", cnt["V"])

        @blk.scalar
        def _(sc, n2=n2, var=var, rs=rs):
            sc.wait_ge(sV, mil[("l1var", n2)])
            mil[("l1std", n2)] = tick(
                sc.activation(rs, var, AF.Sqrt, bias=epsc[0:1, 0:1]), "S")

        @blk.vector
        def _(v, n2=n2, mu=mu, rs=rs, nm=nm):
            v.wait_ge(sS, mil[("l1std", n2)])
            tick(v.reciprocal(rs, rs), "V")
            v.wait_ge(sV, cnt["V"])
            mil[("l1rs", n2)] = tick(
                v.scalar_tensor_tensor(nm, mu, -1.0, rs, ALU.mult, ALU.mult),
                "V")

        @blk.tensor
        def _(t, n2=n2, rs=rs, nm=nm):
            t.wait_ge(sV, mil[("l1rs", n2)])
            wtb(t, n2)
            wtb(t, 2 + n2)
            t.matmul(p[n2][:, 0:NT], ones_r[:1, :], rs, start=True, stop=True)
            inst = t.matmul(p[2 + n2][:, 0:NT], ones_r[:1, :], nm,
                            start=True, stop=True)
            mil[("l1bc", n2)] = tick(inst, "T")
            bank_w[n2] = ("T", cnt["T"])
            bank_w[2 + n2] = ("T", cnt["T"])
    mil["l1bc_done"] = cnt["T"]

    first_y2 = [True]
    for n2 in range(NN2):
        for pi in range(NPAN):
            for dtl in range(NDT):
                dt = pi * NDT + dtl
                ggrp = pi * NDT * NN2 + dtl * NN2 + n2

                @blk.vector
                def _(v, dt=dt, n2=n2, ggrp=ggrp):
                    if first_y2[0]:
                        first_y2[0] = False
                        v.wait_ge(sT, mil["l1bc_done"])
                    v.wait_ge(sS, mil[("gsil", ggrp)])
                    v.wait_ge(sV, cnt["V"])
                    ysl = qbuf[:, dt * TOK + n2 * NT: dt * TOK + (n2 + 1) * NT]
                    tmp = sqscr[:, 0:2 * NT].bitcast(F32)
                    tick(v.tensor_mul(tmp, ysl, p[n2][:, 0:NT]), "V")
                    v.wait_ge(sV, cnt["V"])
                    tick(v.tensor_add(tmp, tmp, p[2 + n2][:, 0:NT]), "V")
                    v.wait_ge(sV, cnt["V"])
                    tick(v.tensor_scalar(tmp, tmp, lngb[:, dt:dt + 1],
                                         lngb[:, KT + dt:KT + dt + 1],
                                         ALU.mult, ALU.add), "V")
                    v.wait_ge(sV, cnt["V"])
                    gsl = bbuf[:, dt * TOK + n2 * NT: dt * TOK + (n2 + 1) * NT]
                    inst = v.tensor_mul(ysl, tmp, gsl)
                    mil[("y2", ggrp)] = tick(inst, "V")
                if n2 == NN2 - 1:
                    mil[("y2dt", dt)] = cnt["V"]
        mil[("y2h", n2)] = cnt["V"]
    mil["y2_done"] = cnt["V"]
    for b in range(4):
        bank_r[b] = ("V", mil["y2_done"])

    def emit_ln2A(ci):
        xsl = (ci % 2) * D
        mc = ci * 8

        @blk.sync
        def _(sy, ci=ci, xsl=xsl):
            if ci < 2:
                sy.wait_ge(sT, mil["l1bc_done"])
            else:
                sy.wait_ge(sV, mil[("l2res", ci - 2)])
            mil[("xld", ci)] = dtick(sy.dma_start(
                AP(sscr, xsl, [[SFREE, 128], [1, D]]),
                AP(x_loc.tensor, ci * 128 * D, [[D, 128], [1, D]])),
                "xld%d" % (ci % 2))

        @blk.vector
        def _(v, ci=ci, xsl=xsl, mc=mc):
            dwait(v, mil[("xld", ci)])
            v.wait_ge(sV, mil[("attn_ci", ci)])
            rsl = bbuf[:, ci * D:(ci + 1) * D]
            inst = v.scalar_tensor_tensor(rsl, rsl, 1.0, sscr[:, xsl:xsl + D],
                                          ALU.mult, ALU.add,
                                          accum_out=mscr[:, mc:mc + 1])
            mil[("l2res", ci)] = tick(inst, "V")

        @blk.vector
        def _(v, ci=ci, mc=mc):
            v.wait_ge(sV, cnt["V"])
            rsl = bbuf[:, ci * D:(ci + 1) * D]
            inst = v.scalar_tensor_tensor(sqscr[:, 0:D], rsl, 1.0, rsl,
                                          ALU.mult, ALU.mult,
                                          accum_out=mscr[:, mc + 1:mc + 2])
            mil[("l2sq", ci)] = tick(inst, "V")


    def emit_ln2B(ci):
        mc = ci * 8

        @blk.vector
        def _(v, ci=ci, mc=mc):
            v.wait_ge(sV, mil[("l2sq", ci)])
            tick(v.tensor_scalar_mul(mscr[:, mc + 2:mc + 3],
                                     mscr[:, mc:mc + 1], 1.0 / D), "V")
            tick(v.tensor_scalar_mul(mscr[:, mc + 3:mc + 4],
                                     mscr[:, mc + 1:mc + 2], 1.0 / D), "V")
            v.wait_ge(sV, cnt["V"])
            tick(v.tensor_mul(mscr[:, mc:mc + 1], mscr[:, mc + 2:mc + 3],
                              mscr[:, mc + 2:mc + 3]), "V")
            v.wait_ge(sV, cnt["V"])
            mil[("l2var", ci)] = tick(
                v.tensor_sub(mscr[:, mc + 3:mc + 4], mscr[:, mc + 3:mc + 4],
                             mscr[:, mc:mc + 1]), "V")

        @blk.scalar
        def _(sc, ci=ci, mc=mc):
            sc.wait_ge(sV, mil[("l2var", ci)])
            mil[("l2std", ci)] = tick(
                sc.activation(mscr[:, mc + 4:mc + 5], mscr[:, mc + 3:mc + 4],
                              AF.Sqrt, bias=epsc[:, 0:1]), "S")

        @blk.vector
        def _(v, ci=ci, mc=mc):
            v.wait_ge(sS, mil[("l2std", ci)])
            tick(v.reciprocal(mscr[:, mc + 4:mc + 5], mscr[:, mc + 4:mc + 5]),
                 "V")
            v.wait_ge(sV, cnt["V"])
            mil[("l2rs", ci)] = tick(
                v.scalar_tensor_tensor(mscr[:, mc + 5:mc + 6],
                                       mscr[:, mc + 2:mc + 3], -1.0,
                                       mscr[:, mc + 4:mc + 5],
                                       ALU.mult, ALU.mult), "V")


    ntg = (KT + 7) // 8
    l2t_hist = []

    def emit_ln2C(ci):
        mc = ci * 8

        @blk.vector
        def _(v, ci=ci, mc=mc):
            v.wait_ge(sV, mil[("l2rs", ci)])
            inst = v.tensor_scalar(bbuf[:, ci * D:(ci + 1) * D],
                                   bbuf[:, ci * D:(ci + 1) * D],
                                   mscr[:, mc + 4:mc + 5],
                                   mscr[:, mc + 5:mc + 6],
                                   ALU.mult, ALU.add)
            mil[("l2nm", ci)] = tick(inst, "V")

        if ci < NCH - 2 and KT >= 8:
            @blk.scalar
            def _(sc, ci=ci):
                if ci == 0:
                    sc.wait_ge(sT, mil[("gpdone", NPAN - 1)])
                sc.wait_ge(sV, mil[("l2nm", ci)])
                for dt in range(KT):
                    mil[("l2tx", ci)] = dtick(sc.dma_start(
                        tbuf[:, dt * TOK + ci * 128: dt * TOK + (ci + 1) * 128],
                        bbuf[:, ci * D + dt * 128: ci * D + (dt + 1) * 128],
                        transpose=True), "xt2")
            return

        for g2 in range(ntg):
            idx0 = len(l2t_hist)
            gpar = idx0 % 2
            l2t_hist.append((ci, g2))

            @blk.tensor
            def _(t, ci=ci, g2=g2, gpar=gpar, idx0=idx0):
                if g2 == 0:
                    t.wait_ge(sV, mil[("l2nm", ci)])
                if idx0 >= 2:
                    t.wait_ge(sV, mil[("l2tev", l2t_hist[idx0 - 2])])
                else:
                    wtb(t, 7 if gpar == 0 else 6)
                bank = tp_bank(gpar)
                inst = None
                for j in range(8):
                    dt = g2 * 8 + j
                    if dt >= KT:
                        break
                    inst = t.transpose(bank[:, j * 128:(j + 1) * 128],
                                       bbuf[:, ci * D + dt * 128:
                                            ci * D + (dt + 1) * 128],
                                       ident[:, :])
                mil[("l2tr", ci, g2)] = tick(inst, "T")

            @blk.vector
            def _(v, ci=ci, g2=g2, gpar=gpar):
                v.wait_ge(sT, mil[("l2tr", ci, g2)])
                bank = tp_bank(gpar)
                inst = None
                for j in range(8):
                    dt = g2 * 8 + j
                    if dt >= KT:
                        break
                    inst = v.tensor_copy(
                        tbuf[:, dt * TOK + ci * 128: dt * TOK + (ci + 1) * 128],
                        bank[:, j * 128:(j + 1) * 128])
                mil[("l2tev", (ci, g2))] = tick(inst, "V")

    wo_banks = [4, 5]
    ogrp = 0
    PH = min(4, NPAN)
    NHALF = NPAN // PH
    for pl in range(PH):
        @blk.gpsimd
        def _(g, pl=pl):
            if pl == 0:
                g.wait_ge(sT, mil[("gpdone", NPAN - 1)])
            mil[("wop", pl)] = panel_dma(g, "o", pl, pl)

    for half in range(NHALF):
        if half >= 1:
            for pl in range(PH):
                @blk.gpsimd
                def _(g, half=half, pl=pl):
                    g.wait_ge(sT, mil[("wo_slot_free", half - 1, pl)])
                    mil[("wop", half * PH + pl)] = panel_dma(
                        g, "o", half * PH + pl, pl)
        for ci in range(NCH):
            n2c = ci // (NCH // NN2)
            for pl in range(PH):
                pi = half * PH + pl
                bk = wo_banks[ogrp % len(wo_banks)]

                @blk.tensor
                def _(t, half=half, ci=ci, pl=pl, pi=pi, bk=bk, n2c=n2c,
                      ogrp=ogrp):
                    if ci == 0:
                        dwait(t, mil[("wop", pi)])
                    wtb(t, bk)
                    for k in range(KT):
                        gk = (k // NDT) * NDT * NN2 + (k % NDT) * NN2 + n2c
                        t.wait_ge(sV, mil[("y2", gk)])
                        inst = t.matmul(
                            p[bk][:, 0:PW],
                            qbuf[:, k * TOK + ci * 128: k * TOK + (ci + 1) * 128],
                            warena[:, pl * WSL + k * PW: pl * WSL + (k + 1) * PW],
                            start=(k == 0), stop=(k == KT - 1))
                    mil[("omm", ogrp)] = tick(inst, "T")
                    bank_w[bk] = ("T", cnt["T"])
                    if ci == NCH - 1:
                        mil[("wo_slot_free", half, pl)] = cnt["T"]

                @blk.vector
                def _(v, ci=ci, pi=pi, bk=bk, ogrp=ogrp):
                    v.wait_ge(sT, mil[("omm", ogrp)])
                    v.wait_ge(sV, mil[("y2dt", ((ci + 1) * D - 1) // TOK)])
                    tick(v.tensor_copy(
                        bbuf[:, ci * D + pi * PW: ci * D + (pi + 1) * PW],
                        p[bk][:, 0:PW]), "V")
                    mil[("oev", ogrp)] = cnt["V"]
                    bank_r[bk] = ("V", cnt["V"])
                    if pi == NPAN - 1:
                        mil[("attn_ci", ci)] = cnt["V"]
                ogrp += 1
                if pi == NPAN - 1:
                    emit_ln2A(ci)
                    if ci >= 1:
                        emit_ln2B(ci - 1)
    emit_ln2B(NCH - 1)
    for ci in range(NCH):
        emit_ln2C(ci)
    mil[("opdone", NPAN - 1)] = cnt["T"]
    mil["attn_done"] = cnt["V"]

    mil["x2T_done"] = cnt["V"]
    mil["x2T_dma"] = ("xt2", ds["xt2"][1])
    bank_r[6] = ("V", mil["x2T_done"])
    bank_r[7] = ("V", mil["x2T_done"])
    if c.stop == "LN2":
        return dbg_finish()

    def emit_ln3A(ci):
        mc = 64 + ci * 8

        @blk.vector
        def _(v, ci=ci, mc=mc):
            v.wait_ge(sV, mil[("dacc", NWIN - 1, ND - 1, ci)])
            inst = v.tensor_reduce(mscr[:, mc:mc + 1],
                                   bbuf[:, ci * D:(ci + 1) * D], AX.X, ALU.add)
            mil[("l3rs", ci)] = tick(inst, "V")

        @blk.vector
        def _(v, ci=ci, mc=mc):
            v.wait_ge(sV, cnt["V"])
            bsl = bbuf[:, ci * D:(ci + 1) * D]
            inst = v.scalar_tensor_tensor(sqscr[:, 0:D], bsl, 1.0, bsl,
                                          ALU.mult, ALU.mult,
                                          accum_out=mscr[:, mc + 1:mc + 2])
            mil[("l3sq", ci)] = tick(inst, "V")


    NSLOT = 2 * WINP
    hslot = []
    for j in range(NSLOT):
        if j < NSLOT // 2:
            hslot.append((qbuf, j * (OTN * TOK)))
        else:
            hslot.append((vbuf, (j - NSLOT // 2) * (OTN * TOK)))

    def gu_dma(g, j, which):
        slot = (j % 2) + (0 if which == "gate" else 2)
        pj = (0 if which == "gate" else NPAIR) + j
        return dtick(g.dma_start(
            AP(warena, slot * GUSL, [[WARENA, 128], [1, GUSL]]),
            AP(wgu_t.tensor, pj * 128 * GUSL, [[GUSL, 128], [1, GUSL]])),
            "gu%d" % slot)

    def wd_dma(g, w, n, sub):
        pj = (w * ND + n) * 2 + sub
        return dtick(g.dma_start(
            AP(warena, WD_OFF + sub * WDSL, [[WARENA, 128], [1, WDSL]]),
            AP(wd_t.tensor, pj * 128 * WDSL, [[WDSL, 128], [1, WDSL]])),
            "wd%d" % sub)

    last_dacc = {}

    def emit_pair(w, jj):
        j = w * WINP + jj

        @blk.gpsimd
        def _(g, w=w, j=j, jj=jj):
            if j >= 2:
                g.wait_ge(sT, mil[("updone", j - 2)])
            else:
                g.wait_ge(sT, mil[("opdone", NPAN - 1)])
            mil[("gu_g", j)] = gu_dma(g, j, "gate")
            mil[("gu_u", j)] = gu_dma(g, j, "up")

        for ot in range(OTN):
            for n2 in range(NN2):
                gg = ot * NN2 + n2
                gbk = gg % 2
                ubk = 2 + gg % 2

                @blk.tensor
                def _(t, w=w, j=j, ot=ot, n2=n2, gg=gg, gbk=gbk):
                    if gg == 0:
                        dwait(t, mil[("gu_g", j)])
                        if j == 0:
                            t.wait_ge(sV, mil["x2T_done"])
                            dwait(t, mil["x2T_dma"])
                    wtb(t, gbk)
                    for k in range(KT):
                        inst = t.matmul(
                            p[gbk][:, 0:NT],
                            warena[:, (j % 2) * GUSL + k * OPW + ot * 128:
                                   (j % 2) * GUSL + k * OPW + (ot + 1) * 128],
                            tbuf[:, k * TOK + n2 * NT:
                                 k * TOK + (n2 + 1) * NT],
                            start=(k == 0), stop=(k == KT - 1))
                    mil[("fgmm", j, gg)] = tick(inst, "T")
                    bank_w[gbk] = ("T", cnt["T"])

                @blk.scalar
                def _(sc, j=j, gg=gg, gbk=gbk):
                    sc.wait_ge(sT, mil[("fgmm", j, gg)])
                    prev = (j, gg - 2) if gg >= 2 else \
                        (j - 1, min(gg % 2 + 2, NGG - 1))
                    k2 = ("fh", prev[0], prev[1])
                    if k2 in mil:
                        sc.wait_ge(sV, mil[k2])
                    inst = sc.activation(
                        sscr[:, SIL_OFF + (gg % 2) * NT:
                             SIL_OFF + (gg % 2 + 1) * NT],
                        p[gbk][:, 0:NT], AF.Silu)
                    mil[("fsil", j, gg)] = tick(inst, "S")
                bank_r[gbk] = ("S", cnt["S"])

                @blk.tensor
                def _(t, j=j, ot=ot, n2=n2, gg=gg, ubk=ubk):
                    if gg == 0:
                        dwait(t, mil[("gu_u", j)])
                    wtb(t, ubk)
                    for k in range(KT):
                        inst = t.matmul(
                            p[ubk][:, 0:NT],
                            warena[:, (2 + j % 2) * GUSL + k * OPW + ot * 128:
                                   (2 + j % 2) * GUSL + k * OPW +
                                   (ot + 1) * 128],
                            tbuf[:, k * TOK + n2 * NT:
                                 k * TOK + (n2 + 1) * NT],
                            start=(k == 0), stop=(k == KT - 1))
                    mil[("fumm", j, gg)] = tick(inst, "T")
                    bank_w[ubk] = ("T", cnt["T"])
                    if gg == NGG - 1:
                        mil[("updone", j)] = cnt["T"]

                @blk.vector
                def _(v, w=w, j=j, jj=jj, ot=ot, n2=n2, gg=gg, ubk=ubk):
                    v.wait_ge(sT, mil[("fumm", j, gg)])
                    v.wait_ge(sS, mil[("fsil", j, gg)])
                    if w >= 2 and gg == 0 and jj == 0:
                        v.wait_ge(sT, mil[("dwin", w - 2)])
                    buf, off = hslot[j % NSLOT]
                    inst = v.tensor_mul(
                        buf[:, off + ot * TOK + n2 * NT:
                            off + ot * TOK + (n2 + 1) * NT],
                        sscr[:, SIL_OFF + (gg % 2) * NT:
                             SIL_OFF + (gg % 2 + 1) * NT],
                        p[ubk][:, 0:NT])
                    mil[("fh", j, gg)] = tick(inst, "V")
                bank_r[ubk] = ("V", cnt["V"])

    def emit_wd(w, n):
        @blk.gpsimd
        def _(g, w=w, n=n):
            if w == 0 and n == 0:
                g.wait_ge(sT, mil[("opdone", NPAN - 1)])
            else:
                pk = ("dnmm", w, n - 1, NCH - 1) if n > 0 else \
                     ("dnmm", w - 1, ND - 1, NCH - 1)
                g.wait_ge(sT, mil[pk])
            mil[("wdp", w, n, 0)] = wd_dma(g, w, n, 0)
            mil[("wdp", w, n, 1)] = wd_dma(g, w, n, 1)

    def emit_down(w, n):
        for ci in range(NCH):
            bk = 4 + ci % 2

            @blk.tensor
            def _(t, w=w, n=n, ci=ci, bk=bk):
                if ci == 0:
                    dwait(t, mil[("wdp", w, n, 0)])
                    t.wait_ge(sV, mil[("fh", w * WINP + WINP - 1, NGG - 1)])
                if bk in last_dacc:
                    t.wait_ge(sV, last_dacc[bk])
                else:
                    wtb(t, bk)
                halfk = WINK // 2
                for kk in range(WINK):
                    if kk == halfk:
                        dwait(t, mil[("wdp", w, n, 1)])
                    jp = kk // OTN
                    ot = kk % OTN
                    buf, off = hslot[(w * WINP + jp) % NSLOT]
                    sub = kk // halfk
                    kloc = kk - sub * halfk
                    inst = t.matmul(
                        p[bk][:, 0:512],
                        buf[:, off + ot * TOK + ci * 128:
                            off + ot * TOK + (ci + 1) * 128],
                        warena[:, WD_OFF + sub * WDSL + kloc * 512:
                               WD_OFF + sub * WDSL + (kloc + 1) * 512],
                        start=(kk == 0), stop=(kk == WINK - 1))
                mil[("dnmm", w, n, ci)] = tick(inst, "T")
                bank_w[bk] = ("T", cnt["T"])
                if n == ND - 1 and ci == NCH - 1:
                    mil[("dwin", w)] = cnt["T"]

            @blk.vector
            def _(v, w=w, n=n, ci=ci, bk=bk):
                v.wait_ge(sT, mil[("dnmm", w, n, ci)])
                asl = bbuf[:, ci * D + n * 512: ci * D + (n + 1) * 512]
                inst = v.tensor_add(asl, asl, p[bk][:, 0:512])
                mil[("dacc", w, n, ci)] = tick(inst, "V")
                last_dacc[bk] = cnt["V"]
                bank_r[bk] = ("V", cnt["V"])
            if w == NWIN - 1 and n == ND - 1:
                emit_ln3A(ci)

    def down_events(w):
        ev = [("dma", w, 0)]
        for n in range(ND):
            ev.append(("down", w, n))
            if n + 1 < ND:
                ev.append(("dma", w, n + 1))
        return ev

    for w in range(NWIN):
        ev = down_events(w - 1) if w >= 1 else []
        npts = len(ev)
        for jj in range(WINP):
            emit_pair(w, jj)
            while ev and len(ev) * WINP > (WINP - 1 - jj) * npts:
                kind, ww, n = ev.pop(0)
                if kind == "dma":
                    emit_wd(ww, n)
                else:
                    emit_down(ww, n)
        for kind, ww, n in ev:
            (emit_wd if kind == "dma" else emit_down)(ww, n)
    for kind, ww, n in down_events(NWIN - 1):
        (emit_wd if kind == "dma" else emit_down)(ww, n)
    mil["ffn_done"] = cnt["V"]
    if c.stop == "FFN":
        return dbg_finish()

    for ci in range(NCH):
        mc = 64 + ci * 8

        @blk.vector
        def _(v, ci=ci, mc=mc):
            v.wait_ge(sV, mil[("l3sq", ci)])
            tick(v.tensor_scalar_mul(mscr[:, mc + 2:mc + 3],
                                     mscr[:, mc:mc + 1], 1.0 / D), "V")
            tick(v.tensor_scalar_mul(mscr[:, mc + 3:mc + 4],
                                     mscr[:, mc + 1:mc + 2], 1.0 / D), "V")
            v.wait_ge(sV, cnt["V"])
            tick(v.tensor_mul(mscr[:, mc:mc + 1], mscr[:, mc + 2:mc + 3],
                              mscr[:, mc + 2:mc + 3]), "V")
            v.wait_ge(sV, cnt["V"])
            mil[("l3var", ci)] = tick(
                v.tensor_sub(mscr[:, mc + 3:mc + 4], mscr[:, mc + 3:mc + 4],
                             mscr[:, mc:mc + 1]), "V")

        @blk.scalar
        def _(sc, ci=ci, mc=mc):
            sc.wait_ge(sV, mil[("l3var", ci)])
            mil[("l3std", ci)] = tick(
                sc.activation(mscr[:, mc + 4:mc + 5], mscr[:, mc + 3:mc + 4],
                              AF.Sqrt, bias=epsc[:, 0:1]), "S")

        @blk.vector
        def _(v, ci=ci, mc=mc):
            v.wait_ge(sS, mil[("l3std", ci)])
            tick(v.reciprocal(mscr[:, mc + 4:mc + 5], mscr[:, mc + 4:mc + 5]),
                 "V")
            v.wait_ge(sV, cnt["V"])
            mil[("l3rs2", ci)] = tick(
                v.scalar_tensor_tensor(mscr[:, mc + 5:mc + 6],
                                       mscr[:, mc + 2:mc + 3], -1.0,
                                       mscr[:, mc + 4:mc + 5],
                                       ALU.mult, ALU.mult), "V")

    for ci in range(NCH):
        mc = 64 + ci * 8
        osl = (ci % 2) * D

        @blk.vector
        def _(v, ci=ci, mc=mc, osl=osl):
            v.wait_ge(sV, mil[("l3rs2", ci)])
            if ci >= 2:
                dwait(v, mil[("ost", ci - 2)])
            inst = v.tensor_scalar(sscr[:, osl:osl + D],
                                   bbuf[:, ci * D:(ci + 1) * D],
                                   mscr[:, mc + 4:mc + 5],
                                   mscr[:, mc + 5:mc + 6],
                                   ALU.mult, ALU.add)
            mil[("l3nm", ci)] = tick(inst, "V")

        @blk.sync
        def _(sy, ci=ci, osl=osl):
            sy.wait_ge(sV, mil[("l3nm", ci)])
            mil[("ost", ci)] = dtick(sy.dma_start(
                AP(out_loc.tensor, ci * 128 * D, [[D, 128], [1, D]]),
                AP(sscr, osl, [[SFREE, 128], [1, D]])), "out%d" % (ci % 2))

    @blk.sync
    def _(sy):
        sy.wait_ge(ds["out0"][0], ds["out0"][1])
        sy.wait_ge(ds["out1"][0], ds["out1"][1])

    es.close()
    return nc


_CFG = Cfg()
_NC_CACHE = {}


def _get_nc(cfg):
    key = (cfg.D, cfg.TOK, cfg.INTER, cfg.WINP)
    if key not in _NC_CACHE:
        _NC_CACHE[key] = build_program(cfg)
    return _NC_CACHE[key]


def _swz_panels(w, PW):
    D = w.shape[0]
    KT, NPAN = D // 128, w.shape[1] // PW
    return np.ascontiguousarray(
        w.reshape(KT, 128, NPAN, PW).transpose(2, 1, 0, 3).reshape(
            NPAN * 128, KT * PW))


def _swz_wd(wd_t, c):
    halfk = c.WINK // 2
    a = wd_t.reshape(c.NWIN, 2, halfk, 128, c.ND, 512)
    return np.ascontiguousarray(
        a.transpose(0, 4, 1, 3, 2, 5).reshape(c.NWIN * c.ND * 2 * 128,
                                              halfk * 512))


def make_in_maps(c, x, wq, wv, wg, wo, wgu_t, wd_t, beta, lng, lnb, n_cores=8):
    wq = _swz_panels(wq, c.PW)
    wv = _swz_panels(wv, c.PW)
    wg = _swz_panels(wg, c.PW)
    wo = _swz_panels(wo, c.PW)
    wgu_t = _swz_panels(wgu_t, c.OPW)
    wd_t = _swz_wd(wd_t, c)
    in_maps = []
    for core in range(n_cores):
        b, half = core // 2, core % 2
        in_maps.append({
            "x_loc": np.ascontiguousarray(x[b, half * c.TOK:(half + 1) * c.TOK]),
            "x_car": np.ascontiguousarray(x[b, 0:c.TOK]),
            "cflag": np.array([[1.0 if half == 1 else 0.0]], dtype=np.float32),
            "wq": wq, "wv": wv, "wg": wg, "wo": wo,
            "wgu_t": wgu_t, "wd_t": wd_t,
            "beta16": beta, "lng_col": lng, "lnb_col": lnb,
        })
    return in_maps


def kernel(L0_attn_x, L0_attn_Wq, L0_attn_Wv, L0_attn_Wg, L0_attn_Wo,
           L0_attn_beta, L0_attn_ln_g, L0_attn_ln_b,
           L0_ffn_w_gate_up, L0_ffn_w_down):
    c = _CFG
    B, S, D = L0_attn_x.shape
    x = np.ascontiguousarray(np.asarray(L0_attn_x, dtype=np.float32))
    wq = np.ascontiguousarray(np.asarray(L0_attn_Wq, dtype=np.float32))
    wv = np.ascontiguousarray(np.asarray(L0_attn_Wv, dtype=np.float32))
    wg = np.ascontiguousarray(np.asarray(L0_attn_Wg, dtype=np.float32))
    wo = np.ascontiguousarray(np.asarray(L0_attn_Wo, dtype=np.float32))
    wgu_t = np.ascontiguousarray(np.asarray(L0_ffn_w_gate_up, dtype=np.float32).T)
    wd_t = np.ascontiguousarray(np.asarray(L0_ffn_w_down, dtype=np.float32).T)
    beta = np.ascontiguousarray(
        np.asarray(L0_attn_beta, dtype=np.float32).reshape(1, c.H))
    lng = np.ascontiguousarray(
        np.asarray(L0_attn_ln_g, dtype=np.float32).reshape(c.KT, 128).T)
    lnb = np.ascontiguousarray(
        np.asarray(L0_attn_ln_b, dtype=np.float32).reshape(c.KT, 128).T)

    nc = _get_nc(c)
    in_maps = make_in_maps(c, x, wq, wv, wg, wo, wgu_t, wd_t, beta, lng, lnb)
    res = run_bass_kernel_spmd(nc, in_maps, core_ids=list(range(8)))
    out = np.empty((B, S, D), dtype=np.float32)
    for core in range(8):
        b, half = core // 2, core % 2
        out[b, half * c.TOK:(half + 1) * c.TOK] = res.results[core]["out_loc"]
    return out


# version 18
# speedup vs baseline: 1.0373x; 1.0373x over previous
import contextlib
import numpy as np

import concourse.bass as bass
import concourse.mybir as mybir
from concourse.bass_utils import run_bass_kernel_spmd

F32 = mybir.dt.float32
BF16 = mybir.dt.bfloat16
AF = mybir.ActivationFunctionType
ALU = mybir.AluOpType
AX = mybir.AxisListType
AP = bass.AP


class Cfg:
    def __init__(self, D=2048, TOK=1024, INTER=8192, WINP=16, stop=None):
        self.stop = stop
        self.D = D
        self.TOK = TOK
        self.INTER = INTER
        self.H = D // 128
        self.KT = D // 128
        self.NCH = TOK // 128
        self.NT = min(512, TOK)
        self.NN2 = TOK // self.NT
        self.PW = min(256, D)
        self.NPAN = D // self.PW
        self.OPW = 128
        self.NPAIR = INTER // self.OPW
        self.WINP = WINP
        self.NWIN = self.NPAIR // self.WINP
        self.WINK = (self.OPW * self.WINP) // 128
        self.OTN = self.OPW // 128
        self.ND = D // 512 if D >= 512 else 1
        self.EPS = 1e-5


def build_program(c: Cfg):
    nc = bass.Bass("TRN2", target_bir_lowering=False)
    D, TOK, INTER = c.D, c.TOK, c.INTER
    H, KT, NCH, NT, NN2 = c.H, c.KT, c.NCH, c.NT, c.NN2
    PW, NPAN, OPW, NPAIR = c.PW, c.NPAN, c.OPW, c.NPAIR
    WINP, NWIN, WINK, ND = c.WINP, c.NWIN, c.WINK, c.ND
    OTN = c.OTN
    HG = min(8, H)
    NHG = H // HG
    NGG = (OPW // 128) * NN2

    x_loc = nc.dram_tensor("x_loc", [TOK, D], F32, kind="ExternalInput").ap()
    x_car = nc.dram_tensor("x_car", [TOK, D], F32, kind="ExternalInput").ap()
    cflag = nc.dram_tensor("cflag", [1, 1], F32, kind="ExternalInput").ap()
    WSL0 = (D // 128) * c.PW
    wq = nc.dram_tensor("wq", [c.NPAN * 128, WSL0], F32, kind="ExternalInput").ap()
    wv = nc.dram_tensor("wv", [c.NPAN * 128, WSL0], F32, kind="ExternalInput").ap()
    wg = nc.dram_tensor("wg", [c.NPAN * 128, WSL0], F32, kind="ExternalInput").ap()
    wo = nc.dram_tensor("wo", [c.NPAN * 128, WSL0], F32, kind="ExternalInput").ap()
    GUSL0 = (D // 128) * c.OPW
    wgu_t = nc.dram_tensor("wgu_t", [2 * c.NPAIR * 128, GUSL0], F32,
                           kind="ExternalInput").ap()
    WDS0 = (c.WINK // 2) * 512
    wd_t = nc.dram_tensor("wd_t", [c.NWIN * c.ND * 2 * 128, WDS0], F32,
                          kind="ExternalInput").ap()
    beta16 = nc.dram_tensor("beta16", [1, H], F32, kind="ExternalInput").ap()
    lng_col = nc.dram_tensor("lng_col", [128, KT], F32, kind="ExternalInput").ap()
    lnb_col = nc.dram_tensor("lnb_col", [128, KT], F32, kind="ExternalInput").ap()
    out_loc = nc.dram_tensor("out_loc", [TOK, D], F32, kind="ExternalOutput").ap()

    es = contextlib.ExitStack()
    sbuf = lambda n, sh, dt: es.enter_context(nc.sbuf_tensor(n, sh, dt))
    psum = lambda n, sh, dt: es.enter_context(nc.psum_tensor(n, sh, dt))
    sem = lambda n: es.enter_context(nc.semaphore(n))

    tbuf = sbuf("tbuf", [128, KT * TOK], BF16)
    bbuf = sbuf("bbuf", [128, KT * TOK], BF16)
    qbuf = sbuf("qbuf", [128, KT * TOK], BF16)
    vbuf = sbuf("vbuf", [128, NCH * D], BF16)
    WSL = KT * PW
    GUSL = KT * OPW
    WDSL = (WINK // 2) * 512
    WD_OFF = 4 * GUSL
    WARENA = max(4 * WSL, WD_OFF + 2 * WDSL)
    warena = sbuf("warena", [128, WARENA], BF16)
    SFREE = max(8 * NT, 2 * D)
    sscr = sbuf("sscr", [128, SFREE], F32)
    decay = sbuf("decay", [128, H * 128], BF16)
    wpow = sbuf("wpow", [128, H * 128], BF16)
    wredT = sbuf("wredT", [128, NCH * H], BF16)
    ident = sbuf("ident", [128, 128], BF16)
    states = sbuf("states", [128, 2 * HG * 128], BF16)
    carry0 = sbuf("carry0", [128, H], F32)
    lngb = sbuf("lngb", [128, 2 * KT], F32)
    llrow = sbuf("llrow", [1, H + 1], F32)
    llbc = sbuf("llbc", [128, H + 1], F32)
    ones_r = sbuf("ones_r", [1, 128], F32)
    onesb_c = sbuf("onesb_c", [128, 1], BF16)
    zT = sbuf("zT", [128, KT * H], BF16)
    sqscr = sbuf("sqscr", [128, max(2 * TOK, D, 4 * NT)], BF16)
    mscr = sbuf("mscr", [128, 128], F32)
    epsc = sbuf("epsc", [128, 1], F32)
    tscr = sbuf("tscr", [128, 256], F32)

    zsb = sscr[0:H, 0:D]
    zbf = sscr[0:H, D:D + D // 2].bitcast(BF16)
    p = [psum(f"p{i}", [128, 512], F32) for i in range(7)]
    ptr = psum("ptr", [128, 1024], BF16)

    RS_OFF = 2 * NT
    NM_OFF = (2 + NN2) * NT
    B_OFF = 6 * NT
    SIL_OFF = 6 * NT

    sT, sV, sS, sP = sem("sT"), sem("sV"), sem("sS"), sem("sP")
    cnt = {"T": 0, "V": 0, "S": 0, "P": 0}
    semmap = {"T": sT, "V": sV, "S": sS, "P": sP}

    def tick(inst, e, k=1):
        inst.then_inc(semmap[e], k)
        cnt[e] += k
        return cnt[e]

    ds = {}
    for dn in ["xt2", "small", "x", "xh0", "xh1", "xld0", "xld1", "w0", "w1", "w2", "w3",
               "gu0", "gu1", "gu2", "gu3", "wd0", "wd1", "out0", "out1"]:
        ds[dn] = [sem("d_" + dn), 0]

    def dtick(inst, name):
        ds[name][1] += 16
        inst.then_inc(ds[name][0], 16)
        return (name, ds[name][1])

    def dwait(eng_obj, key):
        name, v = key
        eng_obj.wait_ge(ds[name][0], v)

    bank_w = {}
    bank_r = {}

    def wtb(eng_obj, idx):
        if idx in bank_r:
            e, n = bank_r[idx]
            eng_obj.wait_ge(semmap[e], n)

    def wrb(eng_obj, idx):
        e, n = bank_w[idx]
        eng_obj.wait_ge(semmap[e], n)

    blk = es.enter_context(nc.Block())
    mil = {}

    def dbg_finish():
        @blk.gpsimd
        def _(g):
            tick(g.memset(sscr[:, 0:D], 0.0), "P")
        @blk.sync
        def _(sy):
            sy.wait_ge(sP, cnt["P"])
            for ci2 in range(NCH):
                dtick(sy.dma_start(
                    AP(out_loc.tensor, ci2 * 128 * D, [[D, 128], [1, D]]),
                    AP(sscr, 0, [[SFREE, 128], [1, D]])), "out0")
            sy.wait_ge(ds["out0"][0], ds["out0"][1])
        es.close()
        return nc

    @blk.sync
    def _(sy):
        dtick(sy.dma_start(llrow[:1, :H], beta16), "small")
        dtick(sy.dma_start(llrow[:1, H:H + 1], cflag), "small")
        dtick(sy.dma_start(lngb[:, 0:KT], lng_col), "small")
        mil["small_in"] = dtick(sy.dma_start(lngb[:, KT:2 * KT], lnb_col),
                                "small")

    @blk.gpsimd
    def _(g):
        h2 = NCH // 2
        mil["x_bf0"] = dtick(g.dma_start(
            AP(bbuf, 0, [[KT * TOK, 128], [D, h2], [1, D]]),
            AP(x_loc.tensor, 0, [[D, 128], [128 * D, h2], [1, D]])), "xh0")
        g.iota(tscr[:, 0:128], [[1, 128]], base=0, channel_multiplier=-1,
               allow_small_or_imprecise_dtypes=True)
        g.iota(tscr[:, 128:256], [[1, 128]], base=1, channel_multiplier=0,
               allow_small_or_imprecise_dtypes=True)
        g.memset(ident[:, :], 1.0)
        g.memset(epsc[:, :], 1e-5)
        g.memset(ones_r[:1, :], 1.0)
        mil["iotas"] = tick(g.memset(onesb_c[:, :], 1.0), "P")
        g.wait_ge(sP, mil["iotas"])
        mil["ident"] = tick(g.affine_select(
            ident[:, :], ident[:, :], [[1, 128]], ALU.is_equal, 0.0,
            base=0, channel_multiplier=-1), "P")
        dwait(g, mil["x_bf0"])
        mil["x_bf"] = dtick(g.dma_start(
            AP(bbuf, h2 * D, [[KT * TOK, 128], [D, NCH - h2], [1, D]]),
            AP(x_loc.tensor, h2 * 128 * D,
               [[D, 128], [128 * D, NCH - h2], [1, D]])), "xh1")


    @blk.scalar
    def _(sc):
        dwait(sc, mil["small_in"])
        tick(sc.activation(llrow[:1, 0:H], llrow[:1, 0:H], AF.Sigmoid), "S")
        sc.wait_ge(sS, cnt["S"])
        mil["loglam"] = tick(sc.activation(llrow[:1, 0:H], llrow[:1, 0:H],
                                           AF.Ln), "S")

    @blk.tensor
    def _(t):
        t.wait_ge(sS, mil["loglam"])
        t.wait_ge(sP, mil["iotas"])
        mil["llmm"] = tick(t.matmul(p[6][:, 0:H + 1], ones_r[:1, :],
                                    llrow[:1, :], start=True, stop=True), "T")
    bank_w[6] = ("T", cnt["T"])

    @blk.vector
    def _(v):
        v.wait_ge(sT, mil["llmm"])
        mil["llbc"] = tick(v.tensor_copy(llbc[:, :], p[6][:, 0:H + 1]), "V")
    bank_r[6] = ("V", cnt["V"])


    tgroups = [(ci, dt) for ci in range(NCH) for dt in range(KT)]
    ngr = (len(tgroups) + 7) // 8

    def tp_bank(par):
        return ptr[:, :] if par == 0 else p[6][:, :].bitcast(BF16)

    for gi in range(ngr):
        tiles = tgroups[gi * 8:(gi + 1) * 8]

        @blk.tensor
        def _(t, gi=gi, tiles=tiles):
            if gi == 0:
                dwait(t, mil["x_bf0"])
                t.wait_ge(sP, mil["ident"])
            if tiles[0][0] >= NCH // 2 and (gi == 0 or tgroups[gi * 8 - 1][0] < NCH // 2):
                dwait(t, mil["x_bf"])
            if gi == 1:
                wtb(t, 6)
            if gi >= 2:
                t.wait_ge(sV, mil[("xev", gi - 2)])
            bank = tp_bank(gi % 2)
            for j, (ci, dt) in enumerate(tiles):
                inst = t.transpose(bank[:, j * 128:(j + 1) * 128],
                                   bbuf[:, ci * D + dt * 128:
                                        ci * D + (dt + 1) * 128],
                                   ident[:, :])
            mil[("xtr", gi)] = tick(inst, "T")

        @blk.vector
        def _(v, gi=gi, tiles=tiles):
            v.wait_ge(sT, mil[("xtr", gi)])
            bank = tp_bank(gi % 2)
            for j, (ci, dt) in enumerate(tiles):
                inst = v.tensor_copy(
                    tbuf[:, dt * TOK + ci * 128: dt * TOK + (ci + 1) * 128],
                    bank[:, j * 128:(j + 1) * 128])
            mil[("xev", gi)] = tick(inst, "V")
    mil["xT_done"] = cnt["V"]
    bank_r[6] = ("V", mil["xT_done"])
    if c.stop == "X":
        return dbg_finish()

    def panel_dma(g, which, pi, slot):
        src = {"q": wq, "v": wv, "g": wg, "o": wo}[which]
        return dtick(g.dma_start(
            AP(warena, slot * WSL, [[WARENA, 128], [1, WSL]]),
            AP(src.tensor, pi * 128 * WSL, [[WSL, 128], [1, WSL]])),
            "w%d" % slot)

    @blk.gpsimd
    def _(g):
        dwait(g, mil["x_bf"])
        for pi in range(min(2, NPAN)):
            mil[("wqp", pi)] = panel_dma(g, "q", pi, pi % 2)
        dwait(g, mil[("wqp", 0)])
        mil["x_car"] = dtick(g.dma_start(
            AP(vbuf, 0, [[NCH * D, 128], [D, NCH], [1, D]]),
            AP(x_car.tensor, 0, [[D, 128], [128 * D, NCH], [1, D]])), "x")

    @blk.vector
    def _(v):
        v.wait_ge(sV, mil["llbc"])
        v.wait_ge(sP, mil["ident"])
        for h in range(H):
            v.tensor_scalar_mul(sscr[:, h * 128:(h + 1) * 128],
                                tscr[:, 0:128], llbc[:, h:h + 1])
            inst = v.tensor_scalar_mul(sscr[:, SFREE // 2 + h * 128:SFREE // 2 + (h + 1) * 128],
                                       tscr[:, 128:256], llbc[:, h:h + 1])
        mil["dmul_all"] = tick(inst, "V")

    @blk.gpsimd
    def _(g):
        g.wait_ge(sV, mil["dmul_all"])
        for h in range(H):
            inst = g.affine_select(sscr[:, h * 128:(h + 1) * 128],
                                   sscr[:, h * 128:(h + 1) * 128], [[1, 128]],
                                   ALU.is_ge, -1e30, base=0,
                                   channel_multiplier=-1)
        mil["dmask_all"] = tick(inst, "P")

    @blk.scalar
    def _(sc):
        sc.wait_ge(sP, mil["dmask_all"])
        for h in range(H):
            sc.activation(decay[:, h * 128:(h + 1) * 128],
                          sscr[:, h * 128:(h + 1) * 128], AF.Exp)
            inst = sc.activation(wpow[:, h * 128:(h + 1) * 128],
                                 sscr[:, SFREE // 2 + h * 128:SFREE // 2 + (h + 1) * 128],
                                 AF.Exp)
        mil["decay_done"] = tick(inst, "S")

    @blk.gpsimd
    def _(g):
        g.wait_ge(sS, mil["decay_done"])
        for st in range(NCH):
            inst = g.iota(tscr[:, st:st + 1], [[1, 1]],
                          base=(TOK - 1) - st * 128, channel_multiplier=-1,
                          allow_small_or_imprecise_dtypes=True)
        mil["rcol_all"] = tick(inst, "P")

    @blk.vector
    def _(v):
        v.wait_ge(sP, mil["rcol_all"])
        for st in range(NCH):
            inst = v.tensor_scalar_mul(sscr[:, st * H:(st + 1) * H],
                                       llbc[:, 0:H], tscr[:, st:st + 1])
        mil["rmul_all"] = tick(inst, "V")

    @blk.scalar
    def _(sc):
        sc.wait_ge(sV, mil["rmul_all"])
        for st in range(NCH):
            inst = sc.activation(sscr[:, SFREE // 2 + st * H:SFREE // 2 + (st + 1) * H],
                                 sscr[:, st * H:(st + 1) * H], AF.Exp)
        mil["rexp_all"] = tick(inst, "S")

    @blk.vector
    def _(v):
        v.wait_ge(sS, mil["rexp_all"])
        for st in range(NCH):
            inst = v.tensor_scalar_mul(wredT[:, st * H:(st + 1) * H],
                                       sscr[:, SFREE // 2 + st * H:SFREE // 2 + (st + 1) * H],
                                       llbc[:, H:H + 1])
        mil["wred_done"] = tick(inst, "V")

    qgrp = 0
    NDT = PW // 128
    for pi in range(NPAN):
        slot = pi % 2
        if pi >= 2:
            @blk.gpsimd
            def _(g, pi=pi, slot=slot):
                g.wait_ge(sT, mil[("qpdone", pi - 2)])
                mil[("wqp", pi)] = panel_dma(g, "q", pi, slot)
        for dtl in range(NDT):
            dt = pi * NDT + dtl
            for n2 in range(NN2):
                bk = qgrp % 4

                @blk.tensor
                def _(t, pi=pi, slot=slot, dt=dt, dtl=dtl, n2=n2, bk=bk,
                      qgrp=qgrp):
                    if dtl == 0 and n2 == 0:
                        dwait(t, mil[("wqp", pi)])
                        if pi == 0:
                            t.wait_ge(sV, mil["xT_done"])
                    wtb(t, bk)
                    for k in range(KT):
                        inst = t.matmul(
                            p[bk][:, 0:NT],
                            warena[:, slot * WSL + k * PW + dtl * 128:
                                   slot * WSL + k * PW + (dtl + 1) * 128],
                            tbuf[:, k * TOK + n2 * NT: k * TOK + (n2 + 1) * NT],
                            start=(k == 0), stop=(k == KT - 1))
                    mil[("qmm", qgrp)] = tick(inst, "T")
                    bank_w[bk] = ("T", cnt["T"])
                    if dtl == NDT - 1 and n2 == NN2 - 1:
                        mil[("qpdone", pi)] = cnt["T"]

                @blk.vector
                def _(v, dt=dt, n2=n2, bk=bk, qgrp=qgrp):
                    v.wait_ge(sT, mil[("qmm", qgrp)])
                    tick(v.tensor_copy(
                        qbuf[:, dt * TOK + n2 * NT: dt * TOK + (n2 + 1) * NT],
                        p[bk][:, 0:NT]), "V")
                    mil[("qev", qgrp)] = cnt["V"]
                    bank_r[bk] = ("V", cnt["V"])
                qgrp += 1
    mil["q_done"] = cnt["V"]
    if c.stop == "Q":
        return dbg_finish()

    for n in range(D // PW):
        @blk.tensor
        def _(t, n=n):
            if n == 0:
                dwait(t, mil["x_car"])
                t.wait_ge(sV, mil["wred_done"])
            wtb(t, 5)
            for st in range(NCH):
                inst = t.matmul(p[5][0:H, 0:PW],
                                wredT[:, st * H:(st + 1) * H],
                                vbuf[:, st * D + n * PW: st * D + (n + 1) * PW],
                                start=(st == 0), stop=(st == NCH - 1))
            mil[("zmm", n)] = tick(inst, "T")
        bank_w[5] = ("T", cnt["T"])

        @blk.vector
        def _(v, n=n):
            v.wait_ge(sT, mil[("zmm", n)])
            tick(v.tensor_copy(zsb[:, n * PW:(n + 1) * PW],
                               p[5][0:H, 0:PW]), "V")
            mil[("zev", n)] = cnt["V"]
        bank_r[5] = ("V", cnt["V"])

    @blk.vector
    def _(v):
        v.wait_ge(sV, cnt["V"])
        tick(v.tensor_copy(zbf[:, :], zsb[:, :]), "V")
        mil["zbf"] = cnt["V"]

    nzg = (KT + 7) // 8
    for zg in range(nzg):
        dts = range(zg * 8, min(zg * 8 + 8, KT))

        @blk.tensor
        def _(t, zg=zg, dts=dts):
            if zg == 0:
                t.wait_ge(sV, mil["zbf"])
            wtb(t, 7 if zg % 2 == 0 else 6)
            if zg >= 2:
                t.wait_ge(sV, mil[("zTev", zg - 2)])
            bank = tp_bank(zg % 2)
            for dt in dts:
                inst = t.transpose(bank[:, (dt % 8) * 128:(dt % 8) * 128 + H],
                                   zbf[:, dt * 128:(dt + 1) * 128],
                                   ident[0:H, 0:H])
            mil[("ztr", zg)] = tick(inst, "T")

        @blk.vector
        def _(v, zg=zg, dts=dts):
            v.wait_ge(sT, mil[("ztr", zg)])
            bank = tp_bank(zg % 2)
            for dt in dts:
                inst = v.tensor_copy(zT[:, dt * H:(dt + 1) * H],
                                     bank[:, (dt % 8) * 128:(dt % 8) * 128 + H])
            mil[("zTev", zg)] = tick(inst, "V")
    mil["zT_done"] = cnt["V"]
    bank_r[7] = ("V", mil["zT_done"])
    bank_r[6] = ("V", mil["zT_done"])

    vgrp = 0
    for pi in range(NPAN):
        slot = 2 + pi % 2

        @blk.gpsimd
        def _(g, pi=pi, slot=slot):
            if pi >= 2:
                g.wait_ge(sT, mil[("vpdone", pi - 2)])
            mil[("wvp", pi)] = panel_dma(g, "v", pi, slot)

        if c.stop != "V0":
            @blk.tensor
            def _(t, pi=pi, slot=slot):
                dwait(t, mil[("wvp", pi)])
                t.wait_ge(sV, mil["zT_done"])
                wtb(t, 5)
                for k in range(KT):
                    inst = t.matmul(p[5][0:H, 0:PW], zT[:, k * H:(k + 1) * H],
                                    warena[:, slot * WSL + k * PW:
                                           slot * WSL + (k + 1) * PW],
                                    start=(k == 0), stop=(k == KT - 1))
                mil[("cmm", pi)] = tick(inst, "T")
                bank_w[5] = ("T", cnt["T"])

            @blk.vector
            def _(v, pi=pi):
                v.wait_ge(sT, mil[("cmm", pi)])
                tick(v.tensor_copy(zsb[:, pi * PW:(pi + 1) * PW],
                                   p[5][0:H, 0:PW]), "V")
                mil[("cev", pi)] = cnt["V"]
                bank_r[5] = ("V", cnt["V"])

        for ci in range(NCH):
            bk = vgrp % 4

            @blk.tensor
            def _(t, pi=pi, slot=slot, ci=ci, bk=bk, vgrp=vgrp):
                wtb(t, bk)
                for k in range(KT):
                    inst = t.matmul(
                        p[bk][:, 0:PW],
                        tbuf[:, k * TOK + ci * 128: k * TOK + (ci + 1) * 128],
                        warena[:, slot * WSL + k * PW: slot * WSL + (k + 1) * PW],
                        start=(k == 0), stop=(k == KT - 1))
                mil[("vmm", vgrp)] = tick(inst, "T")
                bank_w[bk] = ("T", cnt["T"])
                if ci == NCH - 1:
                    mil[("vpdone", pi)] = cnt["T"]

            @blk.vector
            def _(v, pi=pi, ci=ci, bk=bk, vgrp=vgrp):
                v.wait_ge(sT, mil[("vmm", vgrp)])
                tick(v.tensor_copy(
                    vbuf[:, ci * D + pi * PW: ci * D + (pi + 1) * PW],
                    p[bk][:, 0:PW]), "V")
                mil[("vev", vgrp)] = cnt["V"]
                bank_r[bk] = ("V", cnt["V"])
            vgrp += 1
    mil["v_done"] = cnt["V"]
    if c.stop in ("V0", "V1"):
        return dbg_finish()

    @blk.vector
    def _(v):
        v.wait_ge(sV, mil[("cev", NPAN - 1)])
        tick(v.tensor_copy(zbf[:, :], zsb[:, :]), "V")
        mil["carbf"] = cnt["V"]

    ncg = (H + 7) // 8
    for cg in range(ncg):
        hs = range(cg * 8, min(cg * 8 + 8, H))

        @blk.tensor
        def _(t, cg=cg, hs=hs):
            if cg == 0:
                t.wait_ge(sV, mil["carbf"])
            wtb(t, 7 if cg % 2 == 0 else 6)
            if cg >= 2:
                t.wait_ge(sV, mil[("c0ev", cg - 2)])
            bank = tp_bank(cg % 2)
            for h in hs:
                inst = t.transpose(bank[:, (h % 8) * 128:(h % 8) * 128 + H],
                                   zbf[:, h * 128:(h + 1) * 128],
                                   ident[0:H, 0:H])
            mil[("ctr", cg)] = tick(inst, "T")

        @blk.vector
        def _(v, cg=cg, hs=hs):
            v.wait_ge(sT, mil[("ctr", cg)])
            bank = tp_bank(cg % 2)
            for h in hs:
                inst = v.tensor_copy(
                    carry0[:, h:h + 1],
                    bank[:, (h % 8) * 128 + h:(h % 8) * 128 + h + 1])
            mil[("c0ev", cg)] = tick(inst, "V")
    mil["carry0_done"] = cnt["V"]
    bank_r[7] = ("V", mil["carry0_done"])
    bank_r[6] = ("V", mil["carry0_done"])
    if c.stop == "V":
        return dbg_finish()

    Ggroups = [(pi, dtl, n2) for pi in range(NPAN) for dtl in range(NDT)
               for n2 in range(NN2)]
    mil_pool_scan = {}
    gbanks = [0, 1, 6]
    ggrp_idx = [0]

    @blk.gpsimd
    def _(g):
        g.wait_ge(sT, mil[("qpdone", NPAN - 1)])
        mil[("wgp", 0)] = panel_dma(g, "g", 0, 0)
        if NPAN > 1:
            mil[("wgp", 1)] = panel_dma(g, "g", 1, 1)

    def emit_g_group():
        ggrp = ggrp_idx[0]
        if ggrp >= len(Ggroups):
            return
        ggrp_idx[0] += 1
        pi, dtl, n2 = Ggroups[ggrp]
        dt = pi * NDT + dtl
        slot = pi % 2
        bk = gbanks[ggrp % 3]
        first_of_panel = (dtl == 0 and n2 == 0)
        if first_of_panel and pi >= 2:
            @blk.gpsimd
            def _(g, pi=pi, slot=slot):
                g.wait_ge(sT, mil[("gpdone", pi - 2)])
                mil[("wgp", pi)] = panel_dma(g, "g", pi, slot)

        @blk.tensor
        def _(t, pi=pi, slot=slot, dtl=dtl, n2=n2, bk=bk, ggrp=ggrp,
              first_of_panel=first_of_panel):
            if first_of_panel:
                dwait(t, mil[("wgp", pi)])
            wtb(t, bk)
            for k in range(KT):
                inst = t.matmul(
                    p[bk][:, 0:NT],
                    warena[:, slot * WSL + k * PW + dtl * 128:
                           slot * WSL + k * PW + (dtl + 1) * 128],
                    tbuf[:, k * TOK + n2 * NT: k * TOK + (n2 + 1) * NT],
                    start=(k == 0), stop=(k == KT - 1))
            mil[("gmm", ggrp)] = tick(inst, "T")
            bank_w[bk] = ("T", cnt["T"])
            if dtl == NDT - 1 and n2 == NN2 - 1:
                mil[("gpdone", pi)] = cnt["T"]

        @blk.scalar
        def _(sc, dt=dt, n2=n2, bk=bk, ggrp=ggrp):
            sc.wait_ge(sT, mil[("gmm", ggrp)])
            if ggrp == 0:
                sc.wait_ge(sT, mil[("xtr", ngr - 1)])
            inst = sc.activation(
                bbuf[:, dt * TOK + n2 * NT: dt * TOK + (n2 + 1) * NT],
                p[bk][:, 0:NT], AF.Silu)
            mil[("gsil", ggrp)] = tick(inst, "S")
        bank_r[bk] = ("S", cnt["S"])

    for hg in range(NHG):
        for ci in range(NCH):
            par = ci % 2
            bks = (2, 3) if par == 0 else (4, 5)

            @blk.tensor
            def _(t, hg=hg, ci=ci, bks=bks):
                if hg == 0 and ci == 0:
                    t.wait_ge(sV, mil["v_done"])
                    t.wait_ge(sV, mil["carry0_done"])
                if hg == 0 and ci < 2:
                    for b in bks:
                        wtb(t, b)
                if ci >= 2:
                    t.wait_ge(sV, mil[("sdve", hg, ci - 2)])
                elif hg > 0:
                    t.wait_ge(sV, mil[("sdve", hg - 1, NCH - 2 + ci)])
                for hl in range(HG):
                    h = hg * HG + hl
                    bk = bks[hl // 4]
                    inst = t.matmul(
                        p[bk][:, (hl % 4) * 128:(hl % 4 + 1) * 128],
                        vbuf[:, ci * D + h * 128: ci * D + (h + 1) * 128],
                        decay[:, h * 128:(h + 1) * 128],
                        start=True, stop=True)
                mil[("smm", hg, ci)] = tick(inst, "T")
                for b in bks:
                    bank_w[b] = ("T", cnt["T"])

            @blk.vector
            def _(v, hg=hg, ci=ci, par=par, bks=bks):
                v.wait_ge(sT, mil[("smm", hg, ci)])
                if ci > 0:
                    v.wait_ge(sV, mil[("sdve", hg, ci - 1)])
                elif hg > 0:
                    v.wait_ge(sV, mil[("sdve", hg - 1, NCH - 1)])
                if (hg, ci - 2) in mil_pool_scan:
                    v.wait_ge(sV, mil_pool_scan[(hg, ci - 2)])
                elif hg > 0 and ci < 2:
                    v.wait_ge(sV, mil_pool_scan[(hg - 1, NCH - 2 + ci)])
                for hl in range(HG):
                    h = hg * HG + hl
                    bk = bks[hl // 4]
                    st_sl = states[:, (par * HG + hl) * 128:
                                   (par * HG + hl + 1) * 128]
                    if ci == 0:
                        car = carry0[:, h:h + 1]
                    else:
                        car = states[:, ((1 - par) * HG + hl) * 128 + 127:
                                     ((1 - par) * HG + hl) * 128 + 128]
                    inst = v.scalar_tensor_tensor(
                        st_sl, wpow[:, h * 128:(h + 1) * 128], car,
                        p[bk][:, (hl % 4) * 128:(hl % 4 + 1) * 128],
                        ALU.mult, ALU.add)
                tick(inst, "V")
                mil[("sdve", hg, ci)] = cnt["V"]
                for b in bks:
                    bank_r[b] = ("V", cnt["V"])
                v.wait_ge(sV, cnt["V"])
                if hg == 0 and ci == 0:
                    v.wait_ge(sV, mil["q_done"])
                for hl in range(HG):
                    h = hg * HG + hl
                    st_sl = states[:, (par * HG + hl) * 128:
                                   (par * HG + hl + 1) * 128]
                    ysl = qbuf[:, h * TOK + ci * 128: h * TOK + (ci + 1) * 128]
                    inst = v.tensor_mul(ysl, ysl, st_sl)
                mil_pool_scan[(hg, ci)] = tick(inst, "V")

            emit_g_group()
            emit_g_group()
    while ggrp_idx[0] < len(Ggroups):
        emit_g_group()
    mil["y_done"] = mil_pool_scan[(NHG - 1, NCH - 1)]
    mil["gsil_done"] = cnt["S"]
    if c.stop == "SCAN":
        return dbg_finish()

    for h in range(H):
        @blk.vector
        def _(v, h=h):
            if h == 0:
                v.wait_ge(sV, mil["y_done"])
            if h >= 2:
                v.wait_ge(sT, mil[("l1mm", h - 2)])
            ysl = qbuf[:, h * TOK:(h + 1) * TOK]
            inst = v.scalar_tensor_tensor(
                sqscr[:, (h % 2) * TOK:(h % 2) * TOK + TOK],
                ysl, 1.0, ysl, ALU.mult, ALU.mult)
            mil[("l1sq", h)] = tick(inst, "V")

        @blk.tensor
        def _(t, h=h):
            t.wait_ge(sV, mil[("l1sq", h)])
            if h == 0:
                for b in range(4):
                    wtb(t, b)
            for n2 in range(NN2):
                t.matmul(p[n2][0:1, 0:NT], onesb_c[:, 0:1],
                         qbuf[:, h * TOK + n2 * NT: h * TOK + (n2 + 1) * NT],
                         start=(h == 0), stop=(h == H - 1),
                         skip_group_check=True)
                inst = t.matmul(
                    p[2 + n2][0:1, 0:NT], onesb_c[:, 0:1],
                    sqscr[:, (h % 2) * TOK + n2 * NT:
                          (h % 2) * TOK + (n2 + 1) * NT],
                    start=(h == 0), stop=(h == H - 1), skip_group_check=True)
            mil[("l1mm", h)] = tick(inst, "T")
    mil["l1mm_done"] = cnt["T"]
    for b in range(4):
        bank_w[b] = ("T", mil["l1mm_done"])

    for n2 in range(NN2):
        mu = sscr[0:1, n2 * NT:(n2 + 1) * NT]
        var = sscr[0:1, B_OFF + n2 * NT: B_OFF + (n2 + 1) * NT]
        rs = sscr[0:1, RS_OFF + n2 * NT: RS_OFF + (n2 + 1) * NT]
        nm = sscr[0:1, NM_OFF + n2 * NT: NM_OFF + (n2 + 1) * NT]

        @blk.vector
        def _(v, n2=n2, mu=mu, var=var, rs=rs):
            v.wait_ge(sT, mil["l1mm_done"])
            v.wait_ge(sV, cnt["V"])
            tick(v.tensor_scalar_mul(mu, p[n2][0:1, 0:NT], 1.0 / D), "V")
            v.wait_ge(sV, cnt["V"])
            tick(v.tensor_scalar_mul(var, p[2 + n2][0:1, 0:NT], 1.0 / D), "V")
            v.wait_ge(sV, cnt["V"])
            tick(v.tensor_mul(rs, mu, mu), "V")
            v.wait_ge(sV, cnt["V"])
            mil[("l1var", n2)] = tick(v.tensor_sub(var, var, rs), "V")
        bank_r[n2] = ("V", cnt["V"])
        bank_r[2 + n2] = ("V", cnt["V"])

        @blk.scalar
        def _(sc, n2=n2, var=var, rs=rs):
            sc.wait_ge(sV, mil[("l1var", n2)])
            mil[("l1std", n2)] = tick(
                sc.activation(rs, var, AF.Sqrt, bias=epsc[0:1, 0:1]), "S")

        @blk.vector
        def _(v, n2=n2, mu=mu, rs=rs, nm=nm):
            v.wait_ge(sS, mil[("l1std", n2)])
            tick(v.reciprocal(rs, rs), "V")
            v.wait_ge(sV, cnt["V"])
            mil[("l1rs", n2)] = tick(
                v.scalar_tensor_tensor(nm, mu, -1.0, rs, ALU.mult, ALU.mult),
                "V")

        @blk.tensor
        def _(t, n2=n2, rs=rs, nm=nm):
            t.wait_ge(sV, mil[("l1rs", n2)])
            wtb(t, n2)
            wtb(t, 2 + n2)
            t.matmul(p[n2][:, 0:NT], ones_r[:1, :], rs, start=True, stop=True)
            inst = t.matmul(p[2 + n2][:, 0:NT], ones_r[:1, :], nm,
                            start=True, stop=True)
            mil[("l1bc", n2)] = tick(inst, "T")
            bank_w[n2] = ("T", cnt["T"])
            bank_w[2 + n2] = ("T", cnt["T"])
    mil["l1bc_done"] = cnt["T"]

    first_y2 = [True]
    for n2 in range(NN2):
        for pi in range(NPAN):
            for dtl in range(NDT):
                dt = pi * NDT + dtl
                ggrp = pi * NDT * NN2 + dtl * NN2 + n2

                @blk.vector
                def _(v, dt=dt, n2=n2, ggrp=ggrp):
                    if first_y2[0]:
                        first_y2[0] = False
                        v.wait_ge(sT, mil["l1bc_done"])
                    v.wait_ge(sS, mil[("gsil", ggrp)])
                    v.wait_ge(sV, cnt["V"])
                    ysl = qbuf[:, dt * TOK + n2 * NT: dt * TOK + (n2 + 1) * NT]
                    tmp = sqscr[:, 0:2 * NT].bitcast(F32)
                    tick(v.tensor_mul(tmp, ysl, p[n2][:, 0:NT]), "V")
                    v.wait_ge(sV, cnt["V"])
                    tick(v.tensor_add(tmp, tmp, p[2 + n2][:, 0:NT]), "V")
                    v.wait_ge(sV, cnt["V"])
                    tick(v.tensor_scalar(tmp, tmp, lngb[:, dt:dt + 1],
                                         lngb[:, KT + dt:KT + dt + 1],
                                         ALU.mult, ALU.add), "V")
                    v.wait_ge(sV, cnt["V"])
                    gsl = bbuf[:, dt * TOK + n2 * NT: dt * TOK + (n2 + 1) * NT]
                    inst = v.tensor_mul(ysl, tmp, gsl)
                    mil[("y2", ggrp)] = tick(inst, "V")
                if n2 == NN2 - 1:
                    mil[("y2dt", dt)] = cnt["V"]
        mil[("y2h", n2)] = cnt["V"]
    mil["y2_done"] = cnt["V"]
    for b in range(4):
        bank_r[b] = ("V", mil["y2_done"])

    def emit_ln2A(ci):
        xsl = (ci % 2) * D
        mc = ci * 8

        @blk.sync
        def _(sy, ci=ci, xsl=xsl):
            if ci < 2:
                sy.wait_ge(sT, mil["l1bc_done"])
            else:
                sy.wait_ge(sV, mil[("l2res", ci - 2)])
            mil[("xld", ci)] = dtick(sy.dma_start(
                AP(sscr, xsl, [[SFREE, 128], [1, D]]),
                AP(x_loc.tensor, ci * 128 * D, [[D, 128], [1, D]])),
                "xld%d" % (ci % 2))

        @blk.vector
        def _(v, ci=ci, xsl=xsl, mc=mc):
            dwait(v, mil[("xld", ci)])
            v.wait_ge(sV, mil[("attn_ci", ci)])
            rsl = bbuf[:, ci * D:(ci + 1) * D]
            inst = v.scalar_tensor_tensor(rsl, rsl, 1.0, sscr[:, xsl:xsl + D],
                                          ALU.mult, ALU.add,
                                          accum_out=mscr[:, mc:mc + 1])
            mil[("l2res", ci)] = tick(inst, "V")

        @blk.vector
        def _(v, ci=ci, mc=mc):
            v.wait_ge(sV, cnt["V"])
            rsl = bbuf[:, ci * D:(ci + 1) * D]
            inst = v.scalar_tensor_tensor(sqscr[:, 0:D], rsl, 1.0, rsl,
                                          ALU.mult, ALU.mult,
                                          accum_out=mscr[:, mc + 1:mc + 2])
            mil[("l2sq", ci)] = tick(inst, "V")


    def emit_ln2B(ci):
        mc = ci * 8

        @blk.vector
        def _(v, ci=ci, mc=mc):
            v.wait_ge(sV, mil[("l2sq", ci)])
            tick(v.tensor_scalar_mul(mscr[:, mc + 2:mc + 3],
                                     mscr[:, mc:mc + 1], 1.0 / D), "V")
            tick(v.tensor_scalar_mul(mscr[:, mc + 3:mc + 4],
                                     mscr[:, mc + 1:mc + 2], 1.0 / D), "V")
            v.wait_ge(sV, cnt["V"])
            tick(v.tensor_mul(mscr[:, mc:mc + 1], mscr[:, mc + 2:mc + 3],
                              mscr[:, mc + 2:mc + 3]), "V")
            v.wait_ge(sV, cnt["V"])
            mil[("l2var", ci)] = tick(
                v.tensor_sub(mscr[:, mc + 3:mc + 4], mscr[:, mc + 3:mc + 4],
                             mscr[:, mc:mc + 1]), "V")

        @blk.scalar
        def _(sc, ci=ci, mc=mc):
            sc.wait_ge(sV, mil[("l2var", ci)])
            mil[("l2std", ci)] = tick(
                sc.activation(mscr[:, mc + 4:mc + 5], mscr[:, mc + 3:mc + 4],
                              AF.Sqrt, bias=epsc[:, 0:1]), "S")

        @blk.vector
        def _(v, ci=ci, mc=mc):
            v.wait_ge(sS, mil[("l2std", ci)])
            tick(v.reciprocal(mscr[:, mc + 4:mc + 5], mscr[:, mc + 4:mc + 5]),
                 "V")
            v.wait_ge(sV, cnt["V"])
            mil[("l2rs", ci)] = tick(
                v.scalar_tensor_tensor(mscr[:, mc + 5:mc + 6],
                                       mscr[:, mc + 2:mc + 3], -1.0,
                                       mscr[:, mc + 4:mc + 5],
                                       ALU.mult, ALU.mult), "V")


    ntg = (KT + 7) // 8
    l2t_hist = []

    def emit_ln2C(ci):
        mc = ci * 8

        @blk.vector
        def _(v, ci=ci, mc=mc):
            v.wait_ge(sV, mil[("l2rs", ci)])
            inst = v.tensor_scalar(bbuf[:, ci * D:(ci + 1) * D],
                                   bbuf[:, ci * D:(ci + 1) * D],
                                   mscr[:, mc + 4:mc + 5],
                                   mscr[:, mc + 5:mc + 6],
                                   ALU.mult, ALU.add)
            mil[("l2nm", ci)] = tick(inst, "V")


        for g2 in range(ntg):
            idx0 = len(l2t_hist)
            gpar = idx0 % 2
            l2t_hist.append((ci, g2))

            @blk.tensor
            def _(t, ci=ci, g2=g2, gpar=gpar, idx0=idx0):
                if g2 == 0:
                    t.wait_ge(sV, mil[("l2nm", ci)])
                if idx0 >= 2:
                    t.wait_ge(sV, mil[("l2tev", l2t_hist[idx0 - 2])])
                else:
                    wtb(t, 7 if gpar == 0 else 6)
                bank = tp_bank(gpar)
                inst = None
                for j in range(8):
                    dt = g2 * 8 + j
                    if dt >= KT:
                        break
                    inst = t.transpose(bank[:, j * 128:(j + 1) * 128],
                                       bbuf[:, ci * D + dt * 128:
                                            ci * D + (dt + 1) * 128],
                                       ident[:, :])
                mil[("l2tr", ci, g2)] = tick(inst, "T")

            @blk.vector
            def _(v, ci=ci, g2=g2, gpar=gpar):
                v.wait_ge(sT, mil[("l2tr", ci, g2)])
                bank = tp_bank(gpar)
                inst = None
                for j in range(8):
                    dt = g2 * 8 + j
                    if dt >= KT:
                        break
                    inst = v.tensor_copy(
                        tbuf[:, dt * TOK + ci * 128: dt * TOK + (ci + 1) * 128],
                        bank[:, j * 128:(j + 1) * 128])
                mil[("l2tev", (ci, g2))] = tick(inst, "V")

    wo_banks = [4, 5]
    ogrp = 0
    PH = min(4, NPAN)
    NHALF = NPAN // PH
    for pl in range(PH):
        @blk.gpsimd
        def _(g, pl=pl):
            if pl == 0:
                g.wait_ge(sT, mil[("gpdone", NPAN - 1)])
            mil[("wop", pl)] = panel_dma(g, "o", pl, pl)

    for half in range(NHALF):
        if half >= 1:
            for pl in range(PH):
                @blk.gpsimd
                def _(g, half=half, pl=pl):
                    g.wait_ge(sT, mil[("wo_slot_free", half - 1, pl)])
                    mil[("wop", half * PH + pl)] = panel_dma(
                        g, "o", half * PH + pl, pl)
        for ci in range(NCH):
            n2c = ci // (NCH // NN2)
            for pl in range(PH):
                pi = half * PH + pl
                bk = wo_banks[ogrp % len(wo_banks)]

                @blk.tensor
                def _(t, half=half, ci=ci, pl=pl, pi=pi, bk=bk, n2c=n2c,
                      ogrp=ogrp):
                    if ci == 0:
                        dwait(t, mil[("wop", pi)])
                    wtb(t, bk)
                    for k in range(KT):
                        gk = (k // NDT) * NDT * NN2 + (k % NDT) * NN2 + n2c
                        t.wait_ge(sV, mil[("y2", gk)])
                        inst = t.matmul(
                            p[bk][:, 0:PW],
                            qbuf[:, k * TOK + ci * 128: k * TOK + (ci + 1) * 128],
                            warena[:, pl * WSL + k * PW: pl * WSL + (k + 1) * PW],
                            start=(k == 0), stop=(k == KT - 1))
                    mil[("omm", ogrp)] = tick(inst, "T")
                    bank_w[bk] = ("T", cnt["T"])
                    if ci == NCH - 1:
                        mil[("wo_slot_free", half, pl)] = cnt["T"]

                @blk.vector
                def _(v, ci=ci, pi=pi, bk=bk, ogrp=ogrp):
                    v.wait_ge(sT, mil[("omm", ogrp)])
                    v.wait_ge(sV, mil[("y2dt", ((ci + 1) * D - 1) // TOK)])
                    tick(v.tensor_copy(
                        bbuf[:, ci * D + pi * PW: ci * D + (pi + 1) * PW],
                        p[bk][:, 0:PW]), "V")
                    mil[("oev", ogrp)] = cnt["V"]
                    bank_r[bk] = ("V", cnt["V"])
                    if pi == NPAN - 1:
                        mil[("attn_ci", ci)] = cnt["V"]
                ogrp += 1
                if pi == NPAN - 1:
                    emit_ln2A(ci)
                    if ci >= 1:
                        emit_ln2B(ci - 1)
    emit_ln2B(NCH - 1)
    for ci in range(NCH):
        emit_ln2C(ci)
    mil[("opdone", NPAN - 1)] = cnt["T"]
    mil["attn_done"] = cnt["V"]

    mil["x2T_done"] = cnt["V"]
    mil["x2T_dma"] = ("xt2", ds["xt2"][1])
    bank_r[6] = ("V", mil["x2T_done"])
    bank_r[7] = ("V", mil["x2T_done"])
    if c.stop == "LN2":
        return dbg_finish()

    def emit_ln3A(ci):
        mc = 64 + ci * 8

        @blk.vector
        def _(v, ci=ci, mc=mc):
            v.wait_ge(sV, mil[("dacc", NWIN - 1, ND - 1, ci)])
            inst = v.tensor_reduce(mscr[:, mc:mc + 1],
                                   bbuf[:, ci * D:(ci + 1) * D], AX.X, ALU.add)
            mil[("l3rs", ci)] = tick(inst, "V")

        @blk.vector
        def _(v, ci=ci, mc=mc):
            v.wait_ge(sV, cnt["V"])
            bsl = bbuf[:, ci * D:(ci + 1) * D]
            inst = v.scalar_tensor_tensor(sqscr[:, 0:D], bsl, 1.0, bsl,
                                          ALU.mult, ALU.mult,
                                          accum_out=mscr[:, mc + 1:mc + 2])
            mil[("l3sq", ci)] = tick(inst, "V")


    NSLOT = 2 * WINP
    hslot = []
    for j in range(NSLOT):
        if j < NSLOT // 2:
            hslot.append((qbuf, j * (OTN * TOK)))
        else:
            hslot.append((vbuf, (j - NSLOT // 2) * (OTN * TOK)))

    def gu_dma(g, j, which):
        slot = (j % 2) + (0 if which == "gate" else 2)
        pj = (0 if which == "gate" else NPAIR) + j
        return dtick(g.dma_start(
            AP(warena, slot * GUSL, [[WARENA, 128], [1, GUSL]]),
            AP(wgu_t.tensor, pj * 128 * GUSL, [[GUSL, 128], [1, GUSL]])),
            "gu%d" % slot)

    def wd_dma(g, w, n, sub):
        pj = (w * ND + n) * 2 + sub
        return dtick(g.dma_start(
            AP(warena, WD_OFF + sub * WDSL, [[WARENA, 128], [1, WDSL]]),
            AP(wd_t.tensor, pj * 128 * WDSL, [[WDSL, 128], [1, WDSL]])),
            "wd%d" % sub)

    last_dacc = {}

    def emit_pair(w, jj):
        j = w * WINP + jj

        @blk.gpsimd
        def _(g, w=w, j=j, jj=jj):
            if j >= 2:
                g.wait_ge(sT, mil[("updone", j - 2)])
            else:
                g.wait_ge(sT, mil[("opdone", NPAN - 1)])
            mil[("gu_g", j)] = gu_dma(g, j, "gate")
            mil[("gu_u", j)] = gu_dma(g, j, "up")

        for ot in range(OTN):
            for n2 in range(NN2):
                gg = ot * NN2 + n2
                gbk = gg % 2
                ubk = 2 + gg % 2

                @blk.tensor
                def _(t, w=w, j=j, ot=ot, n2=n2, gg=gg, gbk=gbk):
                    if gg == 0:
                        dwait(t, mil[("gu_g", j)])
                        if j == 0:
                            t.wait_ge(sV, mil["x2T_done"])
                    wtb(t, gbk)
                    for k in range(KT):
                        inst = t.matmul(
                            p[gbk][:, 0:NT],
                            warena[:, (j % 2) * GUSL + k * OPW + ot * 128:
                                   (j % 2) * GUSL + k * OPW + (ot + 1) * 128],
                            tbuf[:, k * TOK + n2 * NT:
                                 k * TOK + (n2 + 1) * NT],
                            start=(k == 0), stop=(k == KT - 1))
                    mil[("fgmm", j, gg)] = tick(inst, "T")
                    bank_w[gbk] = ("T", cnt["T"])

                @blk.scalar
                def _(sc, j=j, gg=gg, gbk=gbk):
                    sc.wait_ge(sT, mil[("fgmm", j, gg)])
                    prev = (j, gg - 2) if gg >= 2 else \
                        (j - 1, min(gg % 2 + 2, NGG - 1))
                    k2 = ("fh", prev[0], prev[1])
                    if k2 in mil:
                        sc.wait_ge(sV, mil[k2])
                    inst = sc.activation(
                        sscr[:, SIL_OFF + (gg % 2) * NT:
                             SIL_OFF + (gg % 2 + 1) * NT],
                        p[gbk][:, 0:NT], AF.Silu)
                    mil[("fsil", j, gg)] = tick(inst, "S")
                bank_r[gbk] = ("S", cnt["S"])

                @blk.tensor
                def _(t, j=j, ot=ot, n2=n2, gg=gg, ubk=ubk):
                    if gg == 0:
                        dwait(t, mil[("gu_u", j)])
                    wtb(t, ubk)
                    for k in range(KT):
                        inst = t.matmul(
                            p[ubk][:, 0:NT],
                            warena[:, (2 + j % 2) * GUSL + k * OPW + ot * 128:
                                   (2 + j % 2) * GUSL + k * OPW +
                                   (ot + 1) * 128],
                            tbuf[:, k * TOK + n2 * NT:
                                 k * TOK + (n2 + 1) * NT],
                            start=(k == 0), stop=(k == KT - 1))
                    mil[("fumm", j, gg)] = tick(inst, "T")
                    bank_w[ubk] = ("T", cnt["T"])
                    if gg == NGG - 1:
                        mil[("updone", j)] = cnt["T"]

                @blk.vector
                def _(v, w=w, j=j, jj=jj, ot=ot, n2=n2, gg=gg, ubk=ubk):
                    v.wait_ge(sT, mil[("fumm", j, gg)])
                    v.wait_ge(sS, mil[("fsil", j, gg)])
                    if w >= 2 and gg == 0 and jj == 0:
                        v.wait_ge(sT, mil[("dwin", w - 2)])
                    buf, off = hslot[j % NSLOT]
                    inst = v.tensor_mul(
                        buf[:, off + ot * TOK + n2 * NT:
                            off + ot * TOK + (n2 + 1) * NT],
                        sscr[:, SIL_OFF + (gg % 2) * NT:
                             SIL_OFF + (gg % 2 + 1) * NT],
                        p[ubk][:, 0:NT])
                    mil[("fh", j, gg)] = tick(inst, "V")
                bank_r[ubk] = ("V", cnt["V"])

    def emit_wd(w, n):
        @blk.gpsimd
        def _(g, w=w, n=n):
            if w == 0 and n == 0:
                g.wait_ge(sT, mil[("opdone", NPAN - 1)])
            else:
                pk = ("dnmm", w, n - 1, NCH - 1) if n > 0 else \
                     ("dnmm", w - 1, ND - 1, NCH - 1)
                g.wait_ge(sT, mil[pk])
            mil[("wdp", w, n, 0)] = wd_dma(g, w, n, 0)
            mil[("wdp", w, n, 1)] = wd_dma(g, w, n, 1)

    def emit_down(w, n):
        for ci in range(NCH):
            bk = 4 + ci % 2

            @blk.tensor
            def _(t, w=w, n=n, ci=ci, bk=bk):
                if ci == 0:
                    dwait(t, mil[("wdp", w, n, 0)])
                    t.wait_ge(sV, mil[("fh", w * WINP + WINP - 1, NGG - 1)])
                if bk in last_dacc:
                    t.wait_ge(sV, last_dacc[bk])
                else:
                    wtb(t, bk)
                halfk = WINK // 2
                for kk in range(WINK):
                    if kk == halfk:
                        dwait(t, mil[("wdp", w, n, 1)])
                    jp = kk // OTN
                    ot = kk % OTN
                    buf, off = hslot[(w * WINP + jp) % NSLOT]
                    sub = kk // halfk
                    kloc = kk - sub * halfk
                    inst = t.matmul(
                        p[bk][:, 0:512],
                        buf[:, off + ot * TOK + ci * 128:
                            off + ot * TOK + (ci + 1) * 128],
                        warena[:, WD_OFF + sub * WDSL + kloc * 512:
                               WD_OFF + sub * WDSL + (kloc + 1) * 512],
                        start=(kk == 0), stop=(kk == WINK - 1))
                mil[("dnmm", w, n, ci)] = tick(inst, "T")
                bank_w[bk] = ("T", cnt["T"])
                if n == ND - 1 and ci == NCH - 1:
                    mil[("dwin", w)] = cnt["T"]

            @blk.vector
            def _(v, w=w, n=n, ci=ci, bk=bk):
                v.wait_ge(sT, mil[("dnmm", w, n, ci)])
                asl = bbuf[:, ci * D + n * 512: ci * D + (n + 1) * 512]
                inst = v.tensor_add(asl, asl, p[bk][:, 0:512])
                mil[("dacc", w, n, ci)] = tick(inst, "V")
                last_dacc[bk] = cnt["V"]
                bank_r[bk] = ("V", cnt["V"])
            if w == NWIN - 1 and n == ND - 1:
                emit_ln3A(ci)

    def down_events(w):
        ev = [("dma", w, 0)]
        for n in range(ND):
            ev.append(("down", w, n))
            if n + 1 < ND:
                ev.append(("dma", w, n + 1))
        return ev

    for w in range(NWIN):
        ev = down_events(w - 1) if w >= 1 else []
        npts = len(ev)
        for jj in range(WINP):
            emit_pair(w, jj)
            while ev and len(ev) * WINP > (WINP - 1 - jj) * npts:
                kind, ww, n = ev.pop(0)
                if kind == "dma":
                    emit_wd(ww, n)
                else:
                    emit_down(ww, n)
        for kind, ww, n in ev:
            (emit_wd if kind == "dma" else emit_down)(ww, n)
    for kind, ww, n in down_events(NWIN - 1):
        (emit_wd if kind == "dma" else emit_down)(ww, n)
    mil["ffn_done"] = cnt["V"]
    if c.stop == "FFN":
        return dbg_finish()

    for ci in range(NCH):
        mc = 64 + ci * 8

        @blk.vector
        def _(v, ci=ci, mc=mc):
            v.wait_ge(sV, mil[("l3sq", ci)])
            tick(v.tensor_scalar_mul(mscr[:, mc + 2:mc + 3],
                                     mscr[:, mc:mc + 1], 1.0 / D), "V")
            tick(v.tensor_scalar_mul(mscr[:, mc + 3:mc + 4],
                                     mscr[:, mc + 1:mc + 2], 1.0 / D), "V")
            v.wait_ge(sV, cnt["V"])
            tick(v.tensor_mul(mscr[:, mc:mc + 1], mscr[:, mc + 2:mc + 3],
                              mscr[:, mc + 2:mc + 3]), "V")
            v.wait_ge(sV, cnt["V"])
            mil[("l3var", ci)] = tick(
                v.tensor_sub(mscr[:, mc + 3:mc + 4], mscr[:, mc + 3:mc + 4],
                             mscr[:, mc:mc + 1]), "V")

        @blk.scalar
        def _(sc, ci=ci, mc=mc):
            sc.wait_ge(sV, mil[("l3var", ci)])
            mil[("l3std", ci)] = tick(
                sc.activation(mscr[:, mc + 4:mc + 5], mscr[:, mc + 3:mc + 4],
                              AF.Sqrt, bias=epsc[:, 0:1]), "S")

        @blk.vector
        def _(v, ci=ci, mc=mc):
            v.wait_ge(sS, mil[("l3std", ci)])
            tick(v.reciprocal(mscr[:, mc + 4:mc + 5], mscr[:, mc + 4:mc + 5]),
                 "V")
            v.wait_ge(sV, cnt["V"])
            mil[("l3rs2", ci)] = tick(
                v.scalar_tensor_tensor(mscr[:, mc + 5:mc + 6],
                                       mscr[:, mc + 2:mc + 3], -1.0,
                                       mscr[:, mc + 4:mc + 5],
                                       ALU.mult, ALU.mult), "V")

    for ci in range(NCH):
        mc = 64 + ci * 8
        osl = (ci % 2) * D

        @blk.vector
        def _(v, ci=ci, mc=mc, osl=osl):
            v.wait_ge(sV, mil[("l3rs2", ci)])
            if ci >= 2:
                dwait(v, mil[("ost", ci - 2)])
            inst = v.tensor_scalar(sscr[:, osl:osl + D],
                                   bbuf[:, ci * D:(ci + 1) * D],
                                   mscr[:, mc + 4:mc + 5],
                                   mscr[:, mc + 5:mc + 6],
                                   ALU.mult, ALU.add)
            mil[("l3nm", ci)] = tick(inst, "V")

        @blk.sync
        def _(sy, ci=ci, osl=osl):
            sy.wait_ge(sV, mil[("l3nm", ci)])
            mil[("ost", ci)] = dtick(sy.dma_start(
                AP(out_loc.tensor, ci * 128 * D, [[D, 128], [1, D]]),
                AP(sscr, osl, [[SFREE, 128], [1, D]])), "out%d" % (ci % 2))

    @blk.sync
    def _(sy):
        sy.wait_ge(ds["out0"][0], ds["out0"][1])
        sy.wait_ge(ds["out1"][0], ds["out1"][1])

    es.close()
    return nc


_CFG = Cfg()
_NC_CACHE = {}


def _get_nc(cfg):
    key = (cfg.D, cfg.TOK, cfg.INTER, cfg.WINP)
    if key not in _NC_CACHE:
        _NC_CACHE[key] = build_program(cfg)
    return _NC_CACHE[key]


def _swz_panels(w, PW):
    D = w.shape[0]
    KT, NPAN = D // 128, w.shape[1] // PW
    return np.ascontiguousarray(
        w.reshape(KT, 128, NPAN, PW).transpose(2, 1, 0, 3).reshape(
            NPAN * 128, KT * PW))


def _swz_wd(wd_t, c):
    halfk = c.WINK // 2
    a = wd_t.reshape(c.NWIN, 2, halfk, 128, c.ND, 512)
    return np.ascontiguousarray(
        a.transpose(0, 4, 1, 3, 2, 5).reshape(c.NWIN * c.ND * 2 * 128,
                                              halfk * 512))


def make_in_maps(c, x, wq, wv, wg, wo, wgu_t, wd_t, beta, lng, lnb, n_cores=8):
    wq = _swz_panels(wq, c.PW)
    wv = _swz_panels(wv, c.PW)
    wg = _swz_panels(wg, c.PW)
    wo = _swz_panels(wo, c.PW)
    wgu_t = _swz_panels(wgu_t, c.OPW)
    wd_t = _swz_wd(wd_t, c)
    in_maps = []
    for core in range(n_cores):
        b, half = core // 2, core % 2
        in_maps.append({
            "x_loc": np.ascontiguousarray(x[b, half * c.TOK:(half + 1) * c.TOK]),
            "x_car": np.ascontiguousarray(x[b, 0:c.TOK]),
            "cflag": np.array([[1.0 if half == 1 else 0.0]], dtype=np.float32),
            "wq": wq, "wv": wv, "wg": wg, "wo": wo,
            "wgu_t": wgu_t, "wd_t": wd_t,
            "beta16": beta, "lng_col": lng, "lnb_col": lnb,
        })
    return in_maps


def kernel(L0_attn_x, L0_attn_Wq, L0_attn_Wv, L0_attn_Wg, L0_attn_Wo,
           L0_attn_beta, L0_attn_ln_g, L0_attn_ln_b,
           L0_ffn_w_gate_up, L0_ffn_w_down):
    c = _CFG
    B, S, D = L0_attn_x.shape
    x = np.ascontiguousarray(np.asarray(L0_attn_x, dtype=np.float32))
    wq = np.ascontiguousarray(np.asarray(L0_attn_Wq, dtype=np.float32))
    wv = np.ascontiguousarray(np.asarray(L0_attn_Wv, dtype=np.float32))
    wg = np.ascontiguousarray(np.asarray(L0_attn_Wg, dtype=np.float32))
    wo = np.ascontiguousarray(np.asarray(L0_attn_Wo, dtype=np.float32))
    wgu_t = np.ascontiguousarray(np.asarray(L0_ffn_w_gate_up, dtype=np.float32).T)
    wd_t = np.ascontiguousarray(np.asarray(L0_ffn_w_down, dtype=np.float32).T)
    beta = np.ascontiguousarray(
        np.asarray(L0_attn_beta, dtype=np.float32).reshape(1, c.H))
    lng = np.ascontiguousarray(
        np.asarray(L0_attn_ln_g, dtype=np.float32).reshape(c.KT, 128).T)
    lnb = np.ascontiguousarray(
        np.asarray(L0_attn_ln_b, dtype=np.float32).reshape(c.KT, 128).T)

    nc = _get_nc(c)
    in_maps = make_in_maps(c, x, wq, wv, wg, wo, wgu_t, wd_t, beta, lng, lnb)
    res = run_bass_kernel_spmd(nc, in_maps, core_ids=list(range(8)))
    out = np.empty((B, S, D), dtype=np.float32)
    for core in range(8):
        b, half = core // 2, core % 2
        out[b, half * c.TOK:(half + 1) * c.TOK] = res.results[core]["out_loc"]
    return out


# version 19
# speedup vs baseline: 1.0404x; 1.0030x over previous
import contextlib
import numpy as np

import concourse.bass as bass
import concourse.mybir as mybir
from concourse.bass_utils import run_bass_kernel_spmd

F32 = mybir.dt.float32
BF16 = mybir.dt.bfloat16
AF = mybir.ActivationFunctionType
ALU = mybir.AluOpType
AX = mybir.AxisListType
AP = bass.AP


class Cfg:
    def __init__(self, D=2048, TOK=1024, INTER=8192, WINP=16, stop=None):
        self.stop = stop
        self.D = D
        self.TOK = TOK
        self.INTER = INTER
        self.H = D // 128
        self.KT = D // 128
        self.NCH = TOK // 128
        self.NT = min(512, TOK)
        self.NN2 = TOK // self.NT
        self.PW = min(256, D)
        self.NPAN = D // self.PW
        self.OPW = 128
        self.NPAIR = INTER // self.OPW
        self.WINP = WINP
        self.NWIN = self.NPAIR // self.WINP
        self.WINK = (self.OPW * self.WINP) // 128
        self.OTN = self.OPW // 128
        self.ND = D // 512 if D >= 512 else 1
        self.EPS = 1e-5


def build_program(c: Cfg):
    nc = bass.Bass("TRN2", target_bir_lowering=False)
    D, TOK, INTER = c.D, c.TOK, c.INTER
    H, KT, NCH, NT, NN2 = c.H, c.KT, c.NCH, c.NT, c.NN2
    PW, NPAN, OPW, NPAIR = c.PW, c.NPAN, c.OPW, c.NPAIR
    WINP, NWIN, WINK, ND = c.WINP, c.NWIN, c.WINK, c.ND
    OTN = c.OTN
    HG = min(8, H)
    NHG = H // HG
    NGG = (OPW // 128) * NN2

    x_loc = nc.dram_tensor("x_loc", [TOK, D], F32, kind="ExternalInput").ap()
    x_car = nc.dram_tensor("x_car", [TOK, D], F32, kind="ExternalInput").ap()
    cflag = nc.dram_tensor("cflag", [1, 1], F32, kind="ExternalInput").ap()
    WSL0 = (D // 128) * c.PW
    wq = nc.dram_tensor("wq", [c.NPAN * 128, WSL0], F32, kind="ExternalInput").ap()
    wv = nc.dram_tensor("wv", [c.NPAN * 128, WSL0], F32, kind="ExternalInput").ap()
    wg = nc.dram_tensor("wg", [c.NPAN * 128, WSL0], F32, kind="ExternalInput").ap()
    wo = nc.dram_tensor("wo", [c.NPAN * 128, WSL0], F32, kind="ExternalInput").ap()
    GUSL0 = (D // 128) * c.OPW
    wgu_t = nc.dram_tensor("wgu_t", [2 * c.NPAIR * 128, GUSL0], F32,
                           kind="ExternalInput").ap()
    WDS0 = (c.WINK // 2) * 512
    wd_t = nc.dram_tensor("wd_t", [c.NWIN * c.ND * 2 * 128, WDS0], F32,
                          kind="ExternalInput").ap()
    beta16 = nc.dram_tensor("beta16", [1, H], F32, kind="ExternalInput").ap()
    lng_col = nc.dram_tensor("lng_col", [128, KT], F32, kind="ExternalInput").ap()
    lnb_col = nc.dram_tensor("lnb_col", [128, KT], F32, kind="ExternalInput").ap()
    out_loc = nc.dram_tensor("out_loc", [TOK, D], F32, kind="ExternalOutput").ap()

    es = contextlib.ExitStack()
    sbuf = lambda n, sh, dt: es.enter_context(nc.sbuf_tensor(n, sh, dt))
    psum = lambda n, sh, dt: es.enter_context(nc.psum_tensor(n, sh, dt))
    sem = lambda n: es.enter_context(nc.semaphore(n))

    tbuf = sbuf("tbuf", [128, KT * TOK], BF16)
    bbuf = sbuf("bbuf", [128, KT * TOK], BF16)
    qbuf = sbuf("qbuf", [128, KT * TOK], BF16)
    vbuf = sbuf("vbuf", [128, NCH * D], BF16)
    WSL = KT * PW
    GUSL = KT * OPW
    WDSL = (WINK // 2) * 512
    WD_OFF = 4 * GUSL
    WARENA = max(4 * WSL, WD_OFF + 2 * WDSL)
    warena = sbuf("warena", [128, WARENA], BF16)
    SFREE = max(8 * NT, 2 * D)
    sscr = sbuf("sscr", [128, SFREE], F32)
    decay = sbuf("decay", [128, H * 128], BF16)
    wpow = sbuf("wpow", [128, H * 128], BF16)
    wredT = sbuf("wredT", [128, NCH * H], BF16)
    ident = sbuf("ident", [128, 128], BF16)
    states = sbuf("states", [128, 2 * HG * 128], BF16)
    carry0 = sbuf("carry0", [128, H], F32)
    lngb = sbuf("lngb", [128, 2 * KT], F32)
    llrow = sbuf("llrow", [1, H + 1], F32)
    llbc = sbuf("llbc", [128, H + 1], F32)
    ones_r = sbuf("ones_r", [1, 128], F32)
    onesb_c = sbuf("onesb_c", [128, 1], BF16)
    zT = sbuf("zT", [128, KT * H], BF16)
    sqscr = sbuf("sqscr", [128, max(2 * TOK, D, 4 * NT)], BF16)
    mscr = sbuf("mscr", [128, 128], F32)
    epsc = sbuf("epsc", [128, 1], F32)
    tscr = sbuf("tscr", [128, 256], F32)

    zsb = sscr[0:H, 0:D]
    zbf = sscr[0:H, D:D + D // 2].bitcast(BF16)
    p = [psum(f"p{i}", [128, 512], F32) for i in range(7)]
    ptr = psum("ptr", [128, 1024], BF16)

    RS_OFF = 2 * NT
    NM_OFF = (2 + NN2) * NT
    B_OFF = 6 * NT
    SIL_OFF = 6 * NT

    sT, sV, sS, sP = sem("sT"), sem("sV"), sem("sS"), sem("sP")
    cnt = {"T": 0, "V": 0, "S": 0, "P": 0}
    semmap = {"T": sT, "V": sV, "S": sS, "P": sP}

    def tick(inst, e, k=1):
        inst.then_inc(semmap[e], k)
        cnt[e] += k
        return cnt[e]

    ds = {}
    for dn in ["xt2", "small", "x", "xh0", "xh1", "xld0", "xld1", "w0", "w1", "w2", "w3",
               "gu0", "gu1", "gu2", "gu3", "wd0", "wd1", "out0", "out1", "out2", "out3"]:
        ds[dn] = [sem("d_" + dn), 0]

    def dtick(inst, name):
        ds[name][1] += 16
        inst.then_inc(ds[name][0], 16)
        return (name, ds[name][1])

    def dwait(eng_obj, key):
        name, v = key
        eng_obj.wait_ge(ds[name][0], v)

    bank_w = {}
    bank_r = {}

    def wtb(eng_obj, idx):
        if idx in bank_r:
            e, n = bank_r[idx]
            eng_obj.wait_ge(semmap[e], n)

    def wrb(eng_obj, idx):
        e, n = bank_w[idx]
        eng_obj.wait_ge(semmap[e], n)

    blk = es.enter_context(nc.Block())
    mil = {}

    def dbg_finish():
        @blk.gpsimd
        def _(g):
            tick(g.memset(sscr[:, 0:D], 0.0), "P")
        @blk.sync
        def _(sy):
            sy.wait_ge(sP, cnt["P"])
            for ci2 in range(NCH):
                dtick(sy.dma_start(
                    AP(out_loc.tensor, ci2 * 128 * D, [[D, 128], [1, D]]),
                    AP(sscr, 0, [[SFREE, 128], [1, D]])), "out0")
            sy.wait_ge(ds["out0"][0], ds["out0"][1])
        es.close()
        return nc

    @blk.sync
    def _(sy):
        dtick(sy.dma_start(llrow[:1, :H], beta16), "small")
        dtick(sy.dma_start(llrow[:1, H:H + 1], cflag), "small")
        dtick(sy.dma_start(lngb[:, 0:KT], lng_col), "small")
        mil["small_in"] = dtick(sy.dma_start(lngb[:, KT:2 * KT], lnb_col),
                                "small")

    @blk.gpsimd
    def _(g):
        h2 = NCH // 2
        mil["x_bf0"] = dtick(g.dma_start(
            AP(bbuf, 0, [[KT * TOK, 128], [D, h2], [1, D]]),
            AP(x_loc.tensor, 0, [[D, 128], [128 * D, h2], [1, D]])), "xh0")
        g.iota(tscr[:, 0:128], [[1, 128]], base=0, channel_multiplier=-1,
               allow_small_or_imprecise_dtypes=True)
        g.iota(tscr[:, 128:256], [[1, 128]], base=1, channel_multiplier=0,
               allow_small_or_imprecise_dtypes=True)
        g.memset(ident[:, :], 1.0)
        g.memset(epsc[:, :], 1e-5)
        g.memset(ones_r[:1, :], 1.0)
        mil["iotas"] = tick(g.memset(onesb_c[:, :], 1.0), "P")
        g.wait_ge(sP, mil["iotas"])
        mil["ident"] = tick(g.affine_select(
            ident[:, :], ident[:, :], [[1, 128]], ALU.is_equal, 0.0,
            base=0, channel_multiplier=-1), "P")
        dwait(g, mil["x_bf0"])
        mil["x_bf"] = dtick(g.dma_start(
            AP(bbuf, h2 * D, [[KT * TOK, 128], [D, NCH - h2], [1, D]]),
            AP(x_loc.tensor, h2 * 128 * D,
               [[D, 128], [128 * D, NCH - h2], [1, D]])), "xh1")


    @blk.scalar
    def _(sc):
        dwait(sc, mil["small_in"])
        tick(sc.activation(llrow[:1, 0:H], llrow[:1, 0:H], AF.Sigmoid), "S")
        sc.wait_ge(sS, cnt["S"])
        mil["loglam"] = tick(sc.activation(llrow[:1, 0:H], llrow[:1, 0:H],
                                           AF.Ln), "S")

    @blk.tensor
    def _(t):
        t.wait_ge(sS, mil["loglam"])
        t.wait_ge(sP, mil["iotas"])
        mil["llmm"] = tick(t.matmul(p[6][:, 0:H + 1], ones_r[:1, :],
                                    llrow[:1, :], start=True, stop=True), "T")
    bank_w[6] = ("T", cnt["T"])

    @blk.vector
    def _(v):
        v.wait_ge(sT, mil["llmm"])
        mil["llbc"] = tick(v.tensor_copy(llbc[:, :], p[6][:, 0:H + 1]), "V")
    bank_r[6] = ("V", cnt["V"])


    tgroups = [(ci, dt) for ci in range(NCH) for dt in range(KT)]
    ngr = (len(tgroups) + 7) // 8

    def tp_bank(par):
        return ptr[:, :] if par == 0 else p[6][:, :].bitcast(BF16)

    for gi in range(ngr):
        tiles = tgroups[gi * 8:(gi + 1) * 8]

        @blk.tensor
        def _(t, gi=gi, tiles=tiles):
            if gi == 0:
                dwait(t, mil["x_bf0"])
                t.wait_ge(sP, mil["ident"])
            if tiles[0][0] >= NCH // 2 and (gi == 0 or tgroups[gi * 8 - 1][0] < NCH // 2):
                dwait(t, mil["x_bf"])
            if gi == 1:
                wtb(t, 6)
            if gi >= 2:
                t.wait_ge(sV, mil[("xev", gi - 2)])
            bank = tp_bank(gi % 2)
            for j, (ci, dt) in enumerate(tiles):
                inst = t.transpose(bank[:, j * 128:(j + 1) * 128],
                                   bbuf[:, ci * D + dt * 128:
                                        ci * D + (dt + 1) * 128],
                                   ident[:, :])
            mil[("xtr", gi)] = tick(inst, "T")

        @blk.vector
        def _(v, gi=gi, tiles=tiles):
            v.wait_ge(sT, mil[("xtr", gi)])
            bank = tp_bank(gi % 2)
            for j, (ci, dt) in enumerate(tiles):
                inst = v.tensor_copy(
                    tbuf[:, dt * TOK + ci * 128: dt * TOK + (ci + 1) * 128],
                    bank[:, j * 128:(j + 1) * 128])
            mil[("xev", gi)] = tick(inst, "V")
    mil["xT_done"] = cnt["V"]
    bank_r[6] = ("V", mil["xT_done"])
    if c.stop == "X":
        return dbg_finish()

    def panel_dma(g, which, pi, slot):
        src = {"q": wq, "v": wv, "g": wg, "o": wo}[which]
        return dtick(g.dma_start(
            AP(warena, slot * WSL, [[WARENA, 128], [1, WSL]]),
            AP(src.tensor, pi * 128 * WSL, [[WSL, 128], [1, WSL]])),
            "w%d" % slot)

    @blk.gpsimd
    def _(g):
        dwait(g, mil["x_bf"])
        for pi in range(min(2, NPAN)):
            mil[("wqp", pi)] = panel_dma(g, "q", pi, pi % 2)
        dwait(g, mil[("wqp", 0)])
        mil["x_car"] = dtick(g.dma_start(
            AP(vbuf, 0, [[NCH * D, 128], [D, NCH], [1, D]]),
            AP(x_car.tensor, 0, [[D, 128], [128 * D, NCH], [1, D]])), "x")

    @blk.vector
    def _(v):
        v.wait_ge(sV, mil["llbc"])
        v.wait_ge(sP, mil["ident"])
        for h in range(H):
            v.tensor_scalar_mul(sscr[:, h * 128:(h + 1) * 128],
                                tscr[:, 0:128], llbc[:, h:h + 1])
            inst = v.tensor_scalar_mul(sscr[:, SFREE // 2 + h * 128:SFREE // 2 + (h + 1) * 128],
                                       tscr[:, 128:256], llbc[:, h:h + 1])
        mil["dmul_all"] = tick(inst, "V")

    @blk.gpsimd
    def _(g):
        g.wait_ge(sV, mil["dmul_all"])
        for h in range(H):
            inst = g.affine_select(sscr[:, h * 128:(h + 1) * 128],
                                   sscr[:, h * 128:(h + 1) * 128], [[1, 128]],
                                   ALU.is_ge, -1e30, base=0,
                                   channel_multiplier=-1)
        mil["dmask_all"] = tick(inst, "P")

    @blk.scalar
    def _(sc):
        sc.wait_ge(sP, mil["dmask_all"])
        for h in range(H):
            sc.activation(decay[:, h * 128:(h + 1) * 128],
                          sscr[:, h * 128:(h + 1) * 128], AF.Exp)
            inst = sc.activation(wpow[:, h * 128:(h + 1) * 128],
                                 sscr[:, SFREE // 2 + h * 128:SFREE // 2 + (h + 1) * 128],
                                 AF.Exp)
        mil["decay_done"] = tick(inst, "S")

    @blk.gpsimd
    def _(g):
        g.wait_ge(sS, mil["decay_done"])
        for st in range(NCH):
            inst = g.iota(tscr[:, st:st + 1], [[1, 1]],
                          base=(TOK - 1) - st * 128, channel_multiplier=-1,
                          allow_small_or_imprecise_dtypes=True)
        mil["rcol_all"] = tick(inst, "P")

    @blk.vector
    def _(v):
        v.wait_ge(sP, mil["rcol_all"])
        for st in range(NCH):
            inst = v.tensor_scalar_mul(sscr[:, st * H:(st + 1) * H],
                                       llbc[:, 0:H], tscr[:, st:st + 1])
        mil["rmul_all"] = tick(inst, "V")

    @blk.scalar
    def _(sc):
        sc.wait_ge(sV, mil["rmul_all"])
        for st in range(NCH):
            inst = sc.activation(sscr[:, SFREE // 2 + st * H:SFREE // 2 + (st + 1) * H],
                                 sscr[:, st * H:(st + 1) * H], AF.Exp)
        mil["rexp_all"] = tick(inst, "S")

    @blk.vector
    def _(v):
        v.wait_ge(sS, mil["rexp_all"])
        for st in range(NCH):
            inst = v.tensor_scalar_mul(wredT[:, st * H:(st + 1) * H],
                                       sscr[:, SFREE // 2 + st * H:SFREE // 2 + (st + 1) * H],
                                       llbc[:, H:H + 1])
        mil["wred_done"] = tick(inst, "V")

    qgrp = 0
    NDT = PW // 128
    for pi in range(NPAN):
        slot = pi % 2
        if pi >= 2:
            @blk.gpsimd
            def _(g, pi=pi, slot=slot):
                g.wait_ge(sT, mil[("qpdone", pi - 2)])
                mil[("wqp", pi)] = panel_dma(g, "q", pi, slot)
        for dtl in range(NDT):
            dt = pi * NDT + dtl
            for n2 in range(NN2):
                bk = qgrp % 4

                @blk.tensor
                def _(t, pi=pi, slot=slot, dt=dt, dtl=dtl, n2=n2, bk=bk,
                      qgrp=qgrp):
                    if dtl == 0 and n2 == 0:
                        dwait(t, mil[("wqp", pi)])
                        if pi == 0:
                            t.wait_ge(sV, mil["xT_done"])
                    wtb(t, bk)
                    for k in range(KT):
                        inst = t.matmul(
                            p[bk][:, 0:NT],
                            warena[:, slot * WSL + k * PW + dtl * 128:
                                   slot * WSL + k * PW + (dtl + 1) * 128],
                            tbuf[:, k * TOK + n2 * NT: k * TOK + (n2 + 1) * NT],
                            start=(k == 0), stop=(k == KT - 1))
                    mil[("qmm", qgrp)] = tick(inst, "T")
                    bank_w[bk] = ("T", cnt["T"])
                    if dtl == NDT - 1 and n2 == NN2 - 1:
                        mil[("qpdone", pi)] = cnt["T"]

                @blk.vector
                def _(v, dt=dt, n2=n2, bk=bk, qgrp=qgrp):
                    v.wait_ge(sT, mil[("qmm", qgrp)])
                    tick(v.tensor_copy(
                        qbuf[:, dt * TOK + n2 * NT: dt * TOK + (n2 + 1) * NT],
                        p[bk][:, 0:NT]), "V")
                    mil[("qev", qgrp)] = cnt["V"]
                    bank_r[bk] = ("V", cnt["V"])
                qgrp += 1
    mil["q_done"] = cnt["V"]
    if c.stop == "Q":
        return dbg_finish()

    for n in range(D // PW):
        @blk.tensor
        def _(t, n=n):
            if n == 0:
                dwait(t, mil["x_car"])
                t.wait_ge(sV, mil["wred_done"])
            wtb(t, 5)
            for st in range(NCH):
                inst = t.matmul(p[5][0:H, 0:PW],
                                wredT[:, st * H:(st + 1) * H],
                                vbuf[:, st * D + n * PW: st * D + (n + 1) * PW],
                                start=(st == 0), stop=(st == NCH - 1))
            mil[("zmm", n)] = tick(inst, "T")
        bank_w[5] = ("T", cnt["T"])

        @blk.vector
        def _(v, n=n):
            v.wait_ge(sT, mil[("zmm", n)])
            tick(v.tensor_copy(zsb[:, n * PW:(n + 1) * PW],
                               p[5][0:H, 0:PW]), "V")
            mil[("zev", n)] = cnt["V"]
        bank_r[5] = ("V", cnt["V"])

    @blk.vector
    def _(v):
        v.wait_ge(sV, cnt["V"])
        tick(v.tensor_copy(zbf[:, :], zsb[:, :]), "V")
        mil["zbf"] = cnt["V"]

    nzg = (KT + 7) // 8
    for zg in range(nzg):
        dts = range(zg * 8, min(zg * 8 + 8, KT))

        @blk.tensor
        def _(t, zg=zg, dts=dts):
            if zg == 0:
                t.wait_ge(sV, mil["zbf"])
            wtb(t, 7 if zg % 2 == 0 else 6)
            if zg >= 2:
                t.wait_ge(sV, mil[("zTev", zg - 2)])
            bank = tp_bank(zg % 2)
            for dt in dts:
                inst = t.transpose(bank[:, (dt % 8) * 128:(dt % 8) * 128 + H],
                                   zbf[:, dt * 128:(dt + 1) * 128],
                                   ident[0:H, 0:H])
            mil[("ztr", zg)] = tick(inst, "T")

        @blk.vector
        def _(v, zg=zg, dts=dts):
            v.wait_ge(sT, mil[("ztr", zg)])
            bank = tp_bank(zg % 2)
            for dt in dts:
                inst = v.tensor_copy(zT[:, dt * H:(dt + 1) * H],
                                     bank[:, (dt % 8) * 128:(dt % 8) * 128 + H])
            mil[("zTev", zg)] = tick(inst, "V")
    mil["zT_done"] = cnt["V"]
    bank_r[7] = ("V", mil["zT_done"])
    bank_r[6] = ("V", mil["zT_done"])

    vgrp = 0
    for pi in range(NPAN):
        slot = 2 + pi % 2

        @blk.gpsimd
        def _(g, pi=pi, slot=slot):
            if pi >= 2:
                g.wait_ge(sT, mil[("vpdone", pi - 2)])
            mil[("wvp", pi)] = panel_dma(g, "v", pi, slot)

        if c.stop != "V0":
            @blk.tensor
            def _(t, pi=pi, slot=slot):
                dwait(t, mil[("wvp", pi)])
                t.wait_ge(sV, mil["zT_done"])
                wtb(t, 5)
                for k in range(KT):
                    inst = t.matmul(p[5][0:H, 0:PW], zT[:, k * H:(k + 1) * H],
                                    warena[:, slot * WSL + k * PW:
                                           slot * WSL + (k + 1) * PW],
                                    start=(k == 0), stop=(k == KT - 1))
                mil[("cmm", pi)] = tick(inst, "T")
                bank_w[5] = ("T", cnt["T"])

            @blk.vector
            def _(v, pi=pi):
                v.wait_ge(sT, mil[("cmm", pi)])
                tick(v.tensor_copy(zsb[:, pi * PW:(pi + 1) * PW],
                                   p[5][0:H, 0:PW]), "V")
                mil[("cev", pi)] = cnt["V"]
                bank_r[5] = ("V", cnt["V"])

        for ci in range(NCH):
            bk = vgrp % 4

            @blk.tensor
            def _(t, pi=pi, slot=slot, ci=ci, bk=bk, vgrp=vgrp):
                wtb(t, bk)
                for k in range(KT):
                    inst = t.matmul(
                        p[bk][:, 0:PW],
                        tbuf[:, k * TOK + ci * 128: k * TOK + (ci + 1) * 128],
                        warena[:, slot * WSL + k * PW: slot * WSL + (k + 1) * PW],
                        start=(k == 0), stop=(k == KT - 1))
                mil[("vmm", vgrp)] = tick(inst, "T")
                bank_w[bk] = ("T", cnt["T"])
                if ci == NCH - 1:
                    mil[("vpdone", pi)] = cnt["T"]

            @blk.vector
            def _(v, pi=pi, ci=ci, bk=bk, vgrp=vgrp):
                v.wait_ge(sT, mil[("vmm", vgrp)])
                tick(v.tensor_copy(
                    vbuf[:, ci * D + pi * PW: ci * D + (pi + 1) * PW],
                    p[bk][:, 0:PW]), "V")
                mil[("vev", vgrp)] = cnt["V"]
                bank_r[bk] = ("V", cnt["V"])
            vgrp += 1
    mil["v_done"] = cnt["V"]
    if c.stop in ("V0", "V1"):
        return dbg_finish()

    @blk.vector
    def _(v):
        v.wait_ge(sV, mil[("cev", NPAN - 1)])
        tick(v.tensor_copy(zbf[:, :], zsb[:, :]), "V")
        mil["carbf"] = cnt["V"]

    ncg = (H + 7) // 8
    for cg in range(ncg):
        hs = range(cg * 8, min(cg * 8 + 8, H))

        @blk.tensor
        def _(t, cg=cg, hs=hs):
            if cg == 0:
                t.wait_ge(sV, mil["carbf"])
            wtb(t, 7 if cg % 2 == 0 else 6)
            if cg >= 2:
                t.wait_ge(sV, mil[("c0ev", cg - 2)])
            bank = tp_bank(cg % 2)
            for h in hs:
                inst = t.transpose(bank[:, (h % 8) * 128:(h % 8) * 128 + H],
                                   zbf[:, h * 128:(h + 1) * 128],
                                   ident[0:H, 0:H])
            mil[("ctr", cg)] = tick(inst, "T")

        @blk.vector
        def _(v, cg=cg, hs=hs):
            v.wait_ge(sT, mil[("ctr", cg)])
            bank = tp_bank(cg % 2)
            for h in hs:
                inst = v.tensor_copy(
                    carry0[:, h:h + 1],
                    bank[:, (h % 8) * 128 + h:(h % 8) * 128 + h + 1])
            mil[("c0ev", cg)] = tick(inst, "V")
    mil["carry0_done"] = cnt["V"]
    bank_r[7] = ("V", mil["carry0_done"])
    bank_r[6] = ("V", mil["carry0_done"])
    if c.stop == "V":
        return dbg_finish()

    Ggroups = [(pi, dtl, n2) for pi in range(NPAN) for dtl in range(NDT)
               for n2 in range(NN2)]
    mil_pool_scan = {}
    gbanks = [0, 1, 6]
    ggrp_idx = [0]

    @blk.gpsimd
    def _(g):
        g.wait_ge(sT, mil[("qpdone", NPAN - 1)])
        mil[("wgp", 0)] = panel_dma(g, "g", 0, 0)
        if NPAN > 1:
            mil[("wgp", 1)] = panel_dma(g, "g", 1, 1)

    def emit_g_group():
        ggrp = ggrp_idx[0]
        if ggrp >= len(Ggroups):
            return
        ggrp_idx[0] += 1
        pi, dtl, n2 = Ggroups[ggrp]
        dt = pi * NDT + dtl
        slot = pi % 2
        bk = gbanks[ggrp % 3]
        first_of_panel = (dtl == 0 and n2 == 0)
        if first_of_panel and pi >= 2:
            @blk.gpsimd
            def _(g, pi=pi, slot=slot):
                g.wait_ge(sT, mil[("gpdone", pi - 2)])
                mil[("wgp", pi)] = panel_dma(g, "g", pi, slot)

        @blk.tensor
        def _(t, pi=pi, slot=slot, dtl=dtl, n2=n2, bk=bk, ggrp=ggrp,
              first_of_panel=first_of_panel):
            if first_of_panel:
                dwait(t, mil[("wgp", pi)])
            wtb(t, bk)
            for k in range(KT):
                inst = t.matmul(
                    p[bk][:, 0:NT],
                    warena[:, slot * WSL + k * PW + dtl * 128:
                           slot * WSL + k * PW + (dtl + 1) * 128],
                    tbuf[:, k * TOK + n2 * NT: k * TOK + (n2 + 1) * NT],
                    start=(k == 0), stop=(k == KT - 1))
            mil[("gmm", ggrp)] = tick(inst, "T")
            bank_w[bk] = ("T", cnt["T"])
            if dtl == NDT - 1 and n2 == NN2 - 1:
                mil[("gpdone", pi)] = cnt["T"]

        @blk.scalar
        def _(sc, dt=dt, n2=n2, bk=bk, ggrp=ggrp):
            sc.wait_ge(sT, mil[("gmm", ggrp)])
            if ggrp == 0:
                sc.wait_ge(sT, mil[("xtr", ngr - 1)])
            inst = sc.activation(
                bbuf[:, dt * TOK + n2 * NT: dt * TOK + (n2 + 1) * NT],
                p[bk][:, 0:NT], AF.Silu)
            mil[("gsil", ggrp)] = tick(inst, "S")
        bank_r[bk] = ("S", cnt["S"])

    for hg in range(NHG):
        for ci in range(NCH):
            par = ci % 2
            bks = (2, 3) if par == 0 else (4, 5)

            @blk.tensor
            def _(t, hg=hg, ci=ci, bks=bks):
                if hg == 0 and ci == 0:
                    t.wait_ge(sV, mil["v_done"])
                    t.wait_ge(sV, mil["carry0_done"])
                if hg == 0 and ci < 2:
                    for b in bks:
                        wtb(t, b)
                if ci >= 2:
                    t.wait_ge(sV, mil[("sdve", hg, ci - 2)])
                elif hg > 0:
                    t.wait_ge(sV, mil[("sdve", hg - 1, NCH - 2 + ci)])
                for hl in range(HG):
                    h = hg * HG + hl
                    bk = bks[hl // 4]
                    inst = t.matmul(
                        p[bk][:, (hl % 4) * 128:(hl % 4 + 1) * 128],
                        vbuf[:, ci * D + h * 128: ci * D + (h + 1) * 128],
                        decay[:, h * 128:(h + 1) * 128],
                        start=True, stop=True)
                mil[("smm", hg, ci)] = tick(inst, "T")
                for b in bks:
                    bank_w[b] = ("T", cnt["T"])

            @blk.vector
            def _(v, hg=hg, ci=ci, par=par, bks=bks):
                v.wait_ge(sT, mil[("smm", hg, ci)])
                if ci > 0:
                    v.wait_ge(sV, mil[("sdve", hg, ci - 1)])
                elif hg > 0:
                    v.wait_ge(sV, mil[("sdve", hg - 1, NCH - 1)])
                if (hg, ci - 2) in mil_pool_scan:
                    v.wait_ge(sV, mil_pool_scan[(hg, ci - 2)])
                elif hg > 0 and ci < 2:
                    v.wait_ge(sV, mil_pool_scan[(hg - 1, NCH - 2 + ci)])
                for hl in range(HG):
                    h = hg * HG + hl
                    bk = bks[hl // 4]
                    st_sl = states[:, (par * HG + hl) * 128:
                                   (par * HG + hl + 1) * 128]
                    if ci == 0:
                        car = carry0[:, h:h + 1]
                    else:
                        car = states[:, ((1 - par) * HG + hl) * 128 + 127:
                                     ((1 - par) * HG + hl) * 128 + 128]
                    inst = v.scalar_tensor_tensor(
                        st_sl, wpow[:, h * 128:(h + 1) * 128], car,
                        p[bk][:, (hl % 4) * 128:(hl % 4 + 1) * 128],
                        ALU.mult, ALU.add)
                tick(inst, "V")
                mil[("sdve", hg, ci)] = cnt["V"]
                for b in bks:
                    bank_r[b] = ("V", cnt["V"])
                v.wait_ge(sV, cnt["V"])
                if hg == 0 and ci == 0:
                    v.wait_ge(sV, mil["q_done"])
                for hl in range(HG):
                    h = hg * HG + hl
                    st_sl = states[:, (par * HG + hl) * 128:
                                   (par * HG + hl + 1) * 128]
                    ysl = qbuf[:, h * TOK + ci * 128: h * TOK + (ci + 1) * 128]
                    inst = v.tensor_mul(ysl, ysl, st_sl)
                mil_pool_scan[(hg, ci)] = tick(inst, "V")

            emit_g_group()
            emit_g_group()
    while ggrp_idx[0] < len(Ggroups):
        emit_g_group()
    mil["y_done"] = mil_pool_scan[(NHG - 1, NCH - 1)]
    mil["gsil_done"] = cnt["S"]
    if c.stop == "SCAN":
        return dbg_finish()

    for h in range(H):
        @blk.vector
        def _(v, h=h):
            if h == 0:
                v.wait_ge(sV, mil["y_done"])
            if h >= 2:
                v.wait_ge(sT, mil[("l1mm", h - 2)])
            ysl = qbuf[:, h * TOK:(h + 1) * TOK]
            inst = v.scalar_tensor_tensor(
                sqscr[:, (h % 2) * TOK:(h % 2) * TOK + TOK],
                ysl, 1.0, ysl, ALU.mult, ALU.mult)
            mil[("l1sq", h)] = tick(inst, "V")

        @blk.tensor
        def _(t, h=h):
            t.wait_ge(sV, mil[("l1sq", h)])
            if h == 0:
                for b in range(4):
                    wtb(t, b)
            for n2 in range(NN2):
                t.matmul(p[n2][0:1, 0:NT], onesb_c[:, 0:1],
                         qbuf[:, h * TOK + n2 * NT: h * TOK + (n2 + 1) * NT],
                         start=(h == 0), stop=(h == H - 1),
                         skip_group_check=True)
                inst = t.matmul(
                    p[2 + n2][0:1, 0:NT], onesb_c[:, 0:1],
                    sqscr[:, (h % 2) * TOK + n2 * NT:
                          (h % 2) * TOK + (n2 + 1) * NT],
                    start=(h == 0), stop=(h == H - 1), skip_group_check=True)
            mil[("l1mm", h)] = tick(inst, "T")
    mil["l1mm_done"] = cnt["T"]
    for b in range(4):
        bank_w[b] = ("T", mil["l1mm_done"])

    for n2 in range(NN2):
        mu = sscr[0:1, n2 * NT:(n2 + 1) * NT]
        var = sscr[0:1, B_OFF + n2 * NT: B_OFF + (n2 + 1) * NT]
        rs = sscr[0:1, RS_OFF + n2 * NT: RS_OFF + (n2 + 1) * NT]
        nm = sscr[0:1, NM_OFF + n2 * NT: NM_OFF + (n2 + 1) * NT]

        @blk.vector
        def _(v, n2=n2, mu=mu, var=var, rs=rs):
            v.wait_ge(sT, mil["l1mm_done"])
            v.wait_ge(sV, cnt["V"])
            tick(v.tensor_scalar_mul(mu, p[n2][0:1, 0:NT], 1.0 / D), "V")
            v.wait_ge(sV, cnt["V"])
            tick(v.tensor_scalar_mul(var, p[2 + n2][0:1, 0:NT], 1.0 / D), "V")
            v.wait_ge(sV, cnt["V"])
            tick(v.tensor_mul(rs, mu, mu), "V")
            v.wait_ge(sV, cnt["V"])
            mil[("l1var", n2)] = tick(v.tensor_sub(var, var, rs), "V")
        bank_r[n2] = ("V", cnt["V"])
        bank_r[2 + n2] = ("V", cnt["V"])

        @blk.scalar
        def _(sc, n2=n2, var=var, rs=rs):
            sc.wait_ge(sV, mil[("l1var", n2)])
            mil[("l1std", n2)] = tick(
                sc.activation(rs, var, AF.Sqrt, bias=epsc[0:1, 0:1]), "S")

        @blk.vector
        def _(v, n2=n2, mu=mu, rs=rs, nm=nm):
            v.wait_ge(sS, mil[("l1std", n2)])
            tick(v.reciprocal(rs, rs), "V")
            v.wait_ge(sV, cnt["V"])
            mil[("l1rs", n2)] = tick(
                v.scalar_tensor_tensor(nm, mu, -1.0, rs, ALU.mult, ALU.mult),
                "V")

        @blk.tensor
        def _(t, n2=n2, rs=rs, nm=nm):
            t.wait_ge(sV, mil[("l1rs", n2)])
            wtb(t, n2)
            wtb(t, 2 + n2)
            t.matmul(p[n2][:, 0:NT], ones_r[:1, :], rs, start=True, stop=True)
            inst = t.matmul(p[2 + n2][:, 0:NT], ones_r[:1, :], nm,
                            start=True, stop=True)
            mil[("l1bc", n2)] = tick(inst, "T")
            bank_w[n2] = ("T", cnt["T"])
            bank_w[2 + n2] = ("T", cnt["T"])
    mil["l1bc_done"] = cnt["T"]

    first_y2 = [True]
    for n2 in range(NN2):
        for pi in range(NPAN):
            for dtl in range(NDT):
                dt = pi * NDT + dtl
                ggrp = pi * NDT * NN2 + dtl * NN2 + n2

                @blk.vector
                def _(v, dt=dt, n2=n2, ggrp=ggrp):
                    if first_y2[0]:
                        first_y2[0] = False
                        v.wait_ge(sT, mil["l1bc_done"])
                    v.wait_ge(sS, mil[("gsil", ggrp)])
                    v.wait_ge(sV, cnt["V"])
                    ysl = qbuf[:, dt * TOK + n2 * NT: dt * TOK + (n2 + 1) * NT]
                    tmp = sqscr[:, 0:2 * NT].bitcast(F32)
                    tick(v.tensor_mul(tmp, ysl, p[n2][:, 0:NT]), "V")
                    v.wait_ge(sV, cnt["V"])
                    tick(v.tensor_add(tmp, tmp, p[2 + n2][:, 0:NT]), "V")
                    v.wait_ge(sV, cnt["V"])
                    tick(v.tensor_scalar(tmp, tmp, lngb[:, dt:dt + 1],
                                         lngb[:, KT + dt:KT + dt + 1],
                                         ALU.mult, ALU.add), "V")
                    v.wait_ge(sV, cnt["V"])
                    gsl = bbuf[:, dt * TOK + n2 * NT: dt * TOK + (n2 + 1) * NT]
                    inst = v.tensor_mul(ysl, tmp, gsl)
                    mil[("y2", ggrp)] = tick(inst, "V")
                if n2 == NN2 - 1:
                    mil[("y2dt", dt)] = cnt["V"]
        mil[("y2h", n2)] = cnt["V"]
    mil["y2_done"] = cnt["V"]
    for b in range(4):
        bank_r[b] = ("V", mil["y2_done"])

    def emit_ln2A(ci):
        xsl = (ci % 2) * D
        mc = ci * 8

        @blk.sync
        def _(sy, ci=ci, xsl=xsl):
            if ci < 2:
                sy.wait_ge(sT, mil["l1bc_done"])
            else:
                sy.wait_ge(sV, mil[("l2res", ci - 2)])
            mil[("xld", ci)] = dtick(sy.dma_start(
                AP(sscr, xsl, [[SFREE, 128], [1, D]]),
                AP(x_loc.tensor, ci * 128 * D, [[D, 128], [1, D]])),
                "xld%d" % (ci % 2))

        @blk.vector
        def _(v, ci=ci, xsl=xsl, mc=mc):
            dwait(v, mil[("xld", ci)])
            v.wait_ge(sV, mil[("attn_ci", ci)])
            rsl = bbuf[:, ci * D:(ci + 1) * D]
            inst = v.scalar_tensor_tensor(rsl, rsl, 1.0, sscr[:, xsl:xsl + D],
                                          ALU.mult, ALU.add,
                                          accum_out=mscr[:, mc:mc + 1])
            mil[("l2res", ci)] = tick(inst, "V")

        @blk.vector
        def _(v, ci=ci, mc=mc):
            v.wait_ge(sV, cnt["V"])
            rsl = bbuf[:, ci * D:(ci + 1) * D]
            inst = v.scalar_tensor_tensor(sqscr[:, 0:D], rsl, 1.0, rsl,
                                          ALU.mult, ALU.mult,
                                          accum_out=mscr[:, mc + 1:mc + 2])
            mil[("l2sq", ci)] = tick(inst, "V")


    def emit_ln2B(ci):
        mc = ci * 8

        @blk.vector
        def _(v, ci=ci, mc=mc):
            v.wait_ge(sV, mil[("l2sq", ci)])
            tick(v.tensor_scalar_mul(mscr[:, mc + 2:mc + 3],
                                     mscr[:, mc:mc + 1], 1.0 / D), "V")
            tick(v.tensor_scalar_mul(mscr[:, mc + 3:mc + 4],
                                     mscr[:, mc + 1:mc + 2], 1.0 / D), "V")
            v.wait_ge(sV, cnt["V"])
            tick(v.tensor_mul(mscr[:, mc:mc + 1], mscr[:, mc + 2:mc + 3],
                              mscr[:, mc + 2:mc + 3]), "V")
            v.wait_ge(sV, cnt["V"])
            mil[("l2var", ci)] = tick(
                v.tensor_sub(mscr[:, mc + 3:mc + 4], mscr[:, mc + 3:mc + 4],
                             mscr[:, mc:mc + 1]), "V")

        @blk.scalar
        def _(sc, ci=ci, mc=mc):
            sc.wait_ge(sV, mil[("l2var", ci)])
            mil[("l2std", ci)] = tick(
                sc.activation(mscr[:, mc + 4:mc + 5], mscr[:, mc + 3:mc + 4],
                              AF.Sqrt, bias=epsc[:, 0:1]), "S")

        @blk.vector
        def _(v, ci=ci, mc=mc):
            v.wait_ge(sS, mil[("l2std", ci)])
            tick(v.reciprocal(mscr[:, mc + 4:mc + 5], mscr[:, mc + 4:mc + 5]),
                 "V")
            v.wait_ge(sV, cnt["V"])
            mil[("l2rs", ci)] = tick(
                v.scalar_tensor_tensor(mscr[:, mc + 5:mc + 6],
                                       mscr[:, mc + 2:mc + 3], -1.0,
                                       mscr[:, mc + 4:mc + 5],
                                       ALU.mult, ALU.mult), "V")


    ntg = (KT + 7) // 8
    l2t_hist = []

    def emit_ln2C(ci):
        mc = ci * 8

        @blk.vector
        def _(v, ci=ci, mc=mc):
            v.wait_ge(sV, mil[("l2rs", ci)])
            inst = v.tensor_scalar(bbuf[:, ci * D:(ci + 1) * D],
                                   bbuf[:, ci * D:(ci + 1) * D],
                                   mscr[:, mc + 4:mc + 5],
                                   mscr[:, mc + 5:mc + 6],
                                   ALU.mult, ALU.add)
            mil[("l2nm", ci)] = tick(inst, "V")


        for g2 in range(ntg):
            idx0 = len(l2t_hist)
            gpar = idx0 % 2
            l2t_hist.append((ci, g2))

            @blk.tensor
            def _(t, ci=ci, g2=g2, gpar=gpar, idx0=idx0):
                if g2 == 0:
                    t.wait_ge(sV, mil[("l2nm", ci)])
                if idx0 >= 2:
                    t.wait_ge(sV, mil[("l2tev", l2t_hist[idx0 - 2])])
                else:
                    wtb(t, 7 if gpar == 0 else 6)
                bank = tp_bank(gpar)
                inst = None
                for j in range(8):
                    dt = g2 * 8 + j
                    if dt >= KT:
                        break
                    inst = t.transpose(bank[:, j * 128:(j + 1) * 128],
                                       bbuf[:, ci * D + dt * 128:
                                            ci * D + (dt + 1) * 128],
                                       ident[:, :])
                mil[("l2tr", ci, g2)] = tick(inst, "T")

            @blk.vector
            def _(v, ci=ci, g2=g2, gpar=gpar):
                v.wait_ge(sT, mil[("l2tr", ci, g2)])
                bank = tp_bank(gpar)
                inst = None
                for j in range(8):
                    dt = g2 * 8 + j
                    if dt >= KT:
                        break
                    inst = v.tensor_copy(
                        tbuf[:, dt * TOK + ci * 128: dt * TOK + (ci + 1) * 128],
                        bank[:, j * 128:(j + 1) * 128])
                mil[("l2tev", (ci, g2))] = tick(inst, "V")

    wo_banks = [4, 5]
    ogrp = 0
    PH = min(4, NPAN)
    NHALF = NPAN // PH
    for pl in range(PH):
        @blk.gpsimd
        def _(g, pl=pl):
            if pl == 0:
                g.wait_ge(sT, mil[("gpdone", NPAN - 1)])
            mil[("wop", pl)] = panel_dma(g, "o", pl, pl)

    for half in range(NHALF):
        if half >= 1:
            for pl in range(PH):
                @blk.gpsimd
                def _(g, half=half, pl=pl):
                    g.wait_ge(sT, mil[("wo_slot_free", half - 1, pl)])
                    mil[("wop", half * PH + pl)] = panel_dma(
                        g, "o", half * PH + pl, pl)
        for ci in range(NCH):
            n2c = ci // (NCH // NN2)
            for pl in range(PH):
                pi = half * PH + pl
                bk = wo_banks[ogrp % len(wo_banks)]

                @blk.tensor
                def _(t, half=half, ci=ci, pl=pl, pi=pi, bk=bk, n2c=n2c,
                      ogrp=ogrp):
                    if ci == 0:
                        dwait(t, mil[("wop", pi)])
                    wtb(t, bk)
                    for k in range(KT):
                        gk = (k // NDT) * NDT * NN2 + (k % NDT) * NN2 + n2c
                        t.wait_ge(sV, mil[("y2", gk)])
                        inst = t.matmul(
                            p[bk][:, 0:PW],
                            qbuf[:, k * TOK + ci * 128: k * TOK + (ci + 1) * 128],
                            warena[:, pl * WSL + k * PW: pl * WSL + (k + 1) * PW],
                            start=(k == 0), stop=(k == KT - 1))
                    mil[("omm", ogrp)] = tick(inst, "T")
                    bank_w[bk] = ("T", cnt["T"])
                    if ci == NCH - 1:
                        mil[("wo_slot_free", half, pl)] = cnt["T"]

                @blk.vector
                def _(v, ci=ci, pi=pi, bk=bk, ogrp=ogrp):
                    v.wait_ge(sT, mil[("omm", ogrp)])
                    v.wait_ge(sV, mil[("y2dt", ((ci + 1) * D - 1) // TOK)])
                    tick(v.tensor_copy(
                        bbuf[:, ci * D + pi * PW: ci * D + (pi + 1) * PW],
                        p[bk][:, 0:PW]), "V")
                    mil[("oev", ogrp)] = cnt["V"]
                    bank_r[bk] = ("V", cnt["V"])
                    if pi == NPAN - 1:
                        mil[("attn_ci", ci)] = cnt["V"]
                ogrp += 1
                if pi == NPAN - 1:
                    emit_ln2A(ci)
                    if ci >= 1:
                        emit_ln2B(ci - 1)
    emit_ln2B(NCH - 1)
    for ci in range(NCH):
        emit_ln2C(ci)
    mil[("opdone", NPAN - 1)] = cnt["T"]
    mil["attn_done"] = cnt["V"]

    mil["x2T_done"] = cnt["V"]
    mil["x2T_dma"] = ("xt2", ds["xt2"][1])
    bank_r[6] = ("V", mil["x2T_done"])
    bank_r[7] = ("V", mil["x2T_done"])
    if c.stop == "LN2":
        return dbg_finish()

    def emit_ln3A(ci):
        mc = 64 + ci * 8

        @blk.vector
        def _(v, ci=ci, mc=mc):
            v.wait_ge(sV, mil[("dacc", NWIN - 1, ND - 1, ci)])
            inst = v.tensor_reduce(mscr[:, mc:mc + 1],
                                   bbuf[:, ci * D:(ci + 1) * D], AX.X, ALU.add)
            mil[("l3rs", ci)] = tick(inst, "V")

        @blk.vector
        def _(v, ci=ci, mc=mc):
            v.wait_ge(sV, cnt["V"])
            bsl = bbuf[:, ci * D:(ci + 1) * D]
            inst = v.scalar_tensor_tensor(sqscr[:, 0:D], bsl, 1.0, bsl,
                                          ALU.mult, ALU.mult,
                                          accum_out=mscr[:, mc + 1:mc + 2])
            mil[("l3sq", ci)] = tick(inst, "V")


    def emit_ln3B(ci):
        mc = 64 + ci * 8

        @blk.vector
        def _(v, ci=ci, mc=mc):
            v.wait_ge(sV, mil[("l3sq", ci)])
            tick(v.tensor_scalar_mul(mscr[:, mc + 2:mc + 3],
                                     mscr[:, mc:mc + 1], 1.0 / D), "V")
            tick(v.tensor_scalar_mul(mscr[:, mc + 3:mc + 4],
                                     mscr[:, mc + 1:mc + 2], 1.0 / D), "V")
            v.wait_ge(sV, cnt["V"])
            tick(v.tensor_mul(mscr[:, mc:mc + 1], mscr[:, mc + 2:mc + 3],
                              mscr[:, mc + 2:mc + 3]), "V")
            v.wait_ge(sV, cnt["V"])
            mil[("l3var", ci)] = tick(
                v.tensor_sub(mscr[:, mc + 3:mc + 4], mscr[:, mc + 3:mc + 4],
                             mscr[:, mc:mc + 1]), "V")

        @blk.scalar
        def _(sc, ci=ci, mc=mc):
            sc.wait_ge(sV, mil[("l3var", ci)])
            mil[("l3std", ci)] = tick(
                sc.activation(mscr[:, mc + 4:mc + 5], mscr[:, mc + 3:mc + 4],
                              AF.Sqrt, bias=epsc[:, 0:1]), "S")

        @blk.vector
        def _(v, ci=ci, mc=mc):
            v.wait_ge(sS, mil[("l3std", ci)])
            tick(v.reciprocal(mscr[:, mc + 4:mc + 5], mscr[:, mc + 4:mc + 5]),
                 "V")
            v.wait_ge(sV, cnt["V"])
            mil[("l3rs2", ci)] = tick(
                v.scalar_tensor_tensor(mscr[:, mc + 5:mc + 6],
                                       mscr[:, mc + 2:mc + 3], -1.0,
                                       mscr[:, mc + 4:mc + 5],
                                       ALU.mult, ALU.mult), "V")


    def emit_ln3C(ci):
        mc = 64 + ci * 8
        def oslot_ap(ci):
            if ci % 4 < 2:
                return sscr[:, (ci % 4) * D:(ci % 4) * D + D]
            return warena[:, :].bitcast(F32)[:, (ci % 4 - 2) * D:
                                             (ci % 4 - 2) * D + D]

        @blk.vector
        def _(v, ci=ci, mc=mc):
            v.wait_ge(sV, mil[("l3rs2", ci)])
            if ci >= 4:
                dwait(v, mil[("ost", ci - 4)])
            inst = v.tensor_scalar(oslot_ap(ci),
                                   bbuf[:, ci * D:(ci + 1) * D],
                                   mscr[:, mc + 4:mc + 5],
                                   mscr[:, mc + 5:mc + 6],
                                   ALU.mult, ALU.add)
            mil[("l3nm", ci)] = tick(inst, "V")

        if ci % 2 == 0:
            @blk.sync
            def _(sy, ci=ci):
                sy.wait_ge(sV, mil[("l3nm", ci)])
                mil[("ost", ci)] = dtick(sy.dma_start(
                    AP(out_loc.tensor, ci * 128 * D, [[D, 128], [1, D]]),
                    oslot_ap(ci)), "out%d" % (ci % 4))
        else:
            @blk.scalar
            def _(sc, ci=ci):
                sc.wait_ge(sV, mil[("l3nm", ci)])
                mil[("ost", ci)] = dtick(sc.dma_start(
                    AP(out_loc.tensor, ci * 128 * D, [[D, 128], [1, D]]),
                    oslot_ap(ci)), "out%d" % (ci % 4))


    NSLOT = 2 * WINP
    hslot = []
    for j in range(NSLOT):
        if j < NSLOT // 2:
            hslot.append((qbuf, j * (OTN * TOK)))
        else:
            hslot.append((vbuf, (j - NSLOT // 2) * (OTN * TOK)))

    def gu_dma(g, j, which):
        slot = (j % 2) + (0 if which == "gate" else 2)
        pj = (0 if which == "gate" else NPAIR) + j
        return dtick(g.dma_start(
            AP(warena, slot * GUSL, [[WARENA, 128], [1, GUSL]]),
            AP(wgu_t.tensor, pj * 128 * GUSL, [[GUSL, 128], [1, GUSL]])),
            "gu%d" % slot)

    def wd_dma(g, w, n, sub):
        pj = (w * ND + n) * 2 + sub
        return dtick(g.dma_start(
            AP(warena, WD_OFF + sub * WDSL, [[WARENA, 128], [1, WDSL]]),
            AP(wd_t.tensor, pj * 128 * WDSL, [[WDSL, 128], [1, WDSL]])),
            "wd%d" % sub)

    last_dacc = {}

    def emit_pair(w, jj):
        j = w * WINP + jj

        @blk.gpsimd
        def _(g, w=w, j=j, jj=jj):
            if j >= 2:
                g.wait_ge(sT, mil[("updone", j - 2)])
            else:
                g.wait_ge(sT, mil[("opdone", NPAN - 1)])
            mil[("gu_g", j)] = gu_dma(g, j, "gate")
            mil[("gu_u", j)] = gu_dma(g, j, "up")

        for ot in range(OTN):
            for n2 in range(NN2):
                gg = ot * NN2 + n2
                gbk = gg % 2
                ubk = 2 + gg % 2

                @blk.tensor
                def _(t, w=w, j=j, ot=ot, n2=n2, gg=gg, gbk=gbk):
                    if gg == 0:
                        dwait(t, mil[("gu_g", j)])
                        if j == 0:
                            t.wait_ge(sV, mil["x2T_done"])
                    wtb(t, gbk)
                    for k in range(KT):
                        inst = t.matmul(
                            p[gbk][:, 0:NT],
                            warena[:, (j % 2) * GUSL + k * OPW + ot * 128:
                                   (j % 2) * GUSL + k * OPW + (ot + 1) * 128],
                            tbuf[:, k * TOK + n2 * NT:
                                 k * TOK + (n2 + 1) * NT],
                            start=(k == 0), stop=(k == KT - 1))
                    mil[("fgmm", j, gg)] = tick(inst, "T")
                    bank_w[gbk] = ("T", cnt["T"])

                @blk.scalar
                def _(sc, j=j, gg=gg, gbk=gbk):
                    sc.wait_ge(sT, mil[("fgmm", j, gg)])
                    prev = (j, gg - 2) if gg >= 2 else \
                        (j - 1, min(gg % 2 + 2, NGG - 1))
                    k2 = ("fh", prev[0], prev[1])
                    if k2 in mil:
                        sc.wait_ge(sV, mil[k2])
                    inst = sc.activation(
                        sscr[:, SIL_OFF + (gg % 2) * NT:
                             SIL_OFF + (gg % 2 + 1) * NT],
                        p[gbk][:, 0:NT], AF.Silu)
                    mil[("fsil", j, gg)] = tick(inst, "S")
                bank_r[gbk] = ("S", cnt["S"])

                @blk.tensor
                def _(t, j=j, ot=ot, n2=n2, gg=gg, ubk=ubk):
                    if gg == 0:
                        dwait(t, mil[("gu_u", j)])
                    wtb(t, ubk)
                    for k in range(KT):
                        inst = t.matmul(
                            p[ubk][:, 0:NT],
                            warena[:, (2 + j % 2) * GUSL + k * OPW + ot * 128:
                                   (2 + j % 2) * GUSL + k * OPW +
                                   (ot + 1) * 128],
                            tbuf[:, k * TOK + n2 * NT:
                                 k * TOK + (n2 + 1) * NT],
                            start=(k == 0), stop=(k == KT - 1))
                    mil[("fumm", j, gg)] = tick(inst, "T")
                    bank_w[ubk] = ("T", cnt["T"])
                    if gg == NGG - 1:
                        mil[("updone", j)] = cnt["T"]

                @blk.vector
                def _(v, w=w, j=j, jj=jj, ot=ot, n2=n2, gg=gg, ubk=ubk):
                    v.wait_ge(sT, mil[("fumm", j, gg)])
                    v.wait_ge(sS, mil[("fsil", j, gg)])
                    if w >= 2 and gg == 0 and jj == 0:
                        v.wait_ge(sT, mil[("dwin", w - 2)])
                    buf, off = hslot[j % NSLOT]
                    inst = v.tensor_mul(
                        buf[:, off + ot * TOK + n2 * NT:
                            off + ot * TOK + (n2 + 1) * NT],
                        sscr[:, SIL_OFF + (gg % 2) * NT:
                             SIL_OFF + (gg % 2 + 1) * NT],
                        p[ubk][:, 0:NT])
                    mil[("fh", j, gg)] = tick(inst, "V")
                bank_r[ubk] = ("V", cnt["V"])

    def emit_wd(w, n):
        @blk.gpsimd
        def _(g, w=w, n=n):
            if w == 0 and n == 0:
                g.wait_ge(sT, mil[("opdone", NPAN - 1)])
            else:
                pk = ("dnmm", w, n - 1, NCH - 1) if n > 0 else \
                     ("dnmm", w - 1, ND - 1, NCH - 1)
                g.wait_ge(sT, mil[pk])
            mil[("wdp", w, n, 0)] = wd_dma(g, w, n, 0)
            mil[("wdp", w, n, 1)] = wd_dma(g, w, n, 1)

    def emit_down(w, n):
        for ci in range(NCH):
            bk = 4 + ci % 2

            @blk.tensor
            def _(t, w=w, n=n, ci=ci, bk=bk):
                if ci == 0:
                    dwait(t, mil[("wdp", w, n, 0)])
                    t.wait_ge(sV, mil[("fh", w * WINP + WINP - 1, NGG - 1)])
                if bk in last_dacc:
                    t.wait_ge(sV, last_dacc[bk])
                else:
                    wtb(t, bk)
                halfk = WINK // 2
                for kk in range(WINK):
                    if kk == halfk:
                        dwait(t, mil[("wdp", w, n, 1)])
                    jp = kk // OTN
                    ot = kk % OTN
                    buf, off = hslot[(w * WINP + jp) % NSLOT]
                    sub = kk // halfk
                    kloc = kk - sub * halfk
                    inst = t.matmul(
                        p[bk][:, 0:512],
                        buf[:, off + ot * TOK + ci * 128:
                            off + ot * TOK + (ci + 1) * 128],
                        warena[:, WD_OFF + sub * WDSL + kloc * 512:
                               WD_OFF + sub * WDSL + (kloc + 1) * 512],
                        start=(kk == 0), stop=(kk == WINK - 1))
                mil[("dnmm", w, n, ci)] = tick(inst, "T")
                bank_w[bk] = ("T", cnt["T"])
                if n == ND - 1 and ci == NCH - 1:
                    mil[("dwin", w)] = cnt["T"]

            @blk.vector
            def _(v, w=w, n=n, ci=ci, bk=bk):
                v.wait_ge(sT, mil[("dnmm", w, n, ci)])
                asl = bbuf[:, ci * D + n * 512: ci * D + (n + 1) * 512]
                inst = v.tensor_add(asl, asl, p[bk][:, 0:512])
                mil[("dacc", w, n, ci)] = tick(inst, "V")
                last_dacc[bk] = cnt["V"]
                bank_r[bk] = ("V", cnt["V"])
            if w == NWIN - 1 and n == ND - 1:
                emit_ln3A(ci)
                emit_ln3B(ci)
                emit_ln3C(ci)

    def down_events(w):
        ev = [("dma", w, 0)]
        for n in range(ND):
            ev.append(("down", w, n))
            if n + 1 < ND:
                ev.append(("dma", w, n + 1))
        return ev

    for w in range(NWIN):
        ev = down_events(w - 1) if w >= 1 else []
        npts = len(ev)
        for jj in range(WINP):
            emit_pair(w, jj)
            while ev and len(ev) * WINP > (WINP - 1 - jj) * npts:
                kind, ww, n = ev.pop(0)
                if kind == "dma":
                    emit_wd(ww, n)
                else:
                    emit_down(ww, n)
        for kind, ww, n in ev:
            (emit_wd if kind == "dma" else emit_down)(ww, n)
    for kind, ww, n in down_events(NWIN - 1):
        (emit_wd if kind == "dma" else emit_down)(ww, n)
    mil["ffn_done"] = cnt["V"]
    if c.stop == "FFN":
        return dbg_finish()

    @blk.sync
    def _(sy):
        for dn in ("out0", "out1", "out2", "out3"):
            sy.wait_ge(ds[dn][0], ds[dn][1])

    es.close()
    return nc


_CFG = Cfg()
_NC_CACHE = {}


def _get_nc(cfg):
    key = (cfg.D, cfg.TOK, cfg.INTER, cfg.WINP)
    if key not in _NC_CACHE:
        _NC_CACHE[key] = build_program(cfg)
    return _NC_CACHE[key]


def _swz_panels(w, PW):
    D = w.shape[0]
    KT, NPAN = D // 128, w.shape[1] // PW
    return np.ascontiguousarray(
        w.reshape(KT, 128, NPAN, PW).transpose(2, 1, 0, 3).reshape(
            NPAN * 128, KT * PW))


def _swz_wd(wd_t, c):
    halfk = c.WINK // 2
    a = wd_t.reshape(c.NWIN, 2, halfk, 128, c.ND, 512)
    return np.ascontiguousarray(
        a.transpose(0, 4, 1, 3, 2, 5).reshape(c.NWIN * c.ND * 2 * 128,
                                              halfk * 512))


def make_in_maps(c, x, wq, wv, wg, wo, wgu_t, wd_t, beta, lng, lnb, n_cores=8):
    wq = _swz_panels(wq, c.PW)
    wv = _swz_panels(wv, c.PW)
    wg = _swz_panels(wg, c.PW)
    wo = _swz_panels(wo, c.PW)
    wgu_t = _swz_panels(wgu_t, c.OPW)
    wd_t = _swz_wd(wd_t, c)
    in_maps = []
    for core in range(n_cores):
        b, half = core // 2, core % 2
        in_maps.append({
            "x_loc": np.ascontiguousarray(x[b, half * c.TOK:(half + 1) * c.TOK]),
            "x_car": np.ascontiguousarray(x[b, 0:c.TOK]),
            "cflag": np.array([[1.0 if half == 1 else 0.0]], dtype=np.float32),
            "wq": wq, "wv": wv, "wg": wg, "wo": wo,
            "wgu_t": wgu_t, "wd_t": wd_t,
            "beta16": beta, "lng_col": lng, "lnb_col": lnb,
        })
    return in_maps


def kernel(L0_attn_x, L0_attn_Wq, L0_attn_Wv, L0_attn_Wg, L0_attn_Wo,
           L0_attn_beta, L0_attn_ln_g, L0_attn_ln_b,
           L0_ffn_w_gate_up, L0_ffn_w_down):
    c = _CFG
    B, S, D = L0_attn_x.shape
    x = np.ascontiguousarray(np.asarray(L0_attn_x, dtype=np.float32))
    wq = np.ascontiguousarray(np.asarray(L0_attn_Wq, dtype=np.float32))
    wv = np.ascontiguousarray(np.asarray(L0_attn_Wv, dtype=np.float32))
    wg = np.ascontiguousarray(np.asarray(L0_attn_Wg, dtype=np.float32))
    wo = np.ascontiguousarray(np.asarray(L0_attn_Wo, dtype=np.float32))
    wgu_t = np.ascontiguousarray(np.asarray(L0_ffn_w_gate_up, dtype=np.float32).T)
    wd_t = np.ascontiguousarray(np.asarray(L0_ffn_w_down, dtype=np.float32).T)
    beta = np.ascontiguousarray(
        np.asarray(L0_attn_beta, dtype=np.float32).reshape(1, c.H))
    lng = np.ascontiguousarray(
        np.asarray(L0_attn_ln_g, dtype=np.float32).reshape(c.KT, 128).T)
    lnb = np.ascontiguousarray(
        np.asarray(L0_attn_ln_b, dtype=np.float32).reshape(c.KT, 128).T)

    nc = _get_nc(c)
    in_maps = make_in_maps(c, x, wq, wv, wg, wo, wgu_t, wd_t, beta, lng, lnb)
    res = run_bass_kernel_spmd(nc, in_maps, core_ids=list(range(8)))
    out = np.empty((B, S, D), dtype=np.float32)
    for core in range(8):
        b, half = core // 2, core % 2
        out[b, half * c.TOK:(half + 1) * c.TOK] = res.results[core]["out_loc"]
    return out
